# version 48
# baseline (speedup 1.0000x reference)
import numpy as np
import concourse.bass as bass
import concourse.mybir as mybir
from concourse.bass_utils import run_bass_kernel_spmd
from contextlib import ExitStack

F32 = mybir.dt.float32
BF16 = mybir.dt.bfloat16
AF = mybir.ActivationFunctionType
ALU = mybir.AluOpType
AX = mybir.AxisListType

B_LOC = 4
DEBUG_DUMP = False
T = 2048
C = 1024
NH = 16
HS = 64
LAT = 512
TG = 512
NGRP = T // TG
NTIL = TG // 128


class Buf:
    __slots__ = ("last_write", "readers")

    def __init__(self):
        self.last_write = None
        self.readers = []


class Prog:
    ENGINES = ("tensor", "vector", "scalar", "sync", "gpsimd")

    def __init__(self):
        self.by_eng = {e: [] for e in self.ENGINES}
        self.nid = 0
        self.semval = {}
        self.maxw = {e: {} for e in self.ENGINES}

    def op(self, eng, fn, reads=(), writes=(), dma=False, accum=False, phase=0):
        deps = set()
        for b in reads:
            if b.last_write is not None:
                deps.add(b.last_write)
        if not accum:
            for b in writes:
                if b.last_write is not None:
                    deps.add(b.last_write)
                deps.update(b.readers)
        i = self.nid
        self.nid += 1
        self.by_eng[eng].append(
            dict(id=i, fn=fn, deps=deps, dma=dma, phase=phase))
        for b in writes:
            b.last_write = i
            if not accum:
                b.readers = []
        for b in reads:
            b.readers.append(i)
        return i

    def finalize(self, extra_targets=()):
        targets = set(extra_targets)
        for recs in self.by_eng.values():
            for r in recs:
                targets |= r["deps"]
        self.sem_totals = {}
        for eng, recs in self.by_eng.items():
            counts = {False: 0, True: 0}
            for r in recs:
                if r["id"] in targets:
                    counts[r["dma"]] += 16 if r["dma"] else 1
                    self.semval[r["id"]] = ((eng, r["dma"]), counts[r["dma"]])
            self.sem_totals[(eng, False)] = counts[False]
            self.sem_totals[(eng, True)] = counts[True]

    def runner(self, eng, phase, sems):
        recs = [r for r in self.by_eng[eng] if r["phase"] == phase]
        semval = self.semval
        maxw = self.maxw[eng]

        def run(q):
            for r in recs:
                for d in sorted(r["deps"]):
                    sv = semval.get(d)
                    if sv is None:
                        continue
                    k, v = sv
                    if maxw.get(k, 0) < v:
                        maxw[k] = v
                        q.wait_ge(sems[k], v)
                ins = r["fn"](q)
                sv = semval.get(r["id"])
                if sv is not None and ins is not None:
                    k, _ = sv
                    ins.then_inc(sems[k], 16 if r["dma"] else 1)
        return run


def build_nc():
    nc = bass.Bass("TRN2", target_bir_lowering=False)

    x_d = nc.dram_tensor("x", [B_LOC, T, C], F32, kind="ExternalInput").ap()
    out_d = nc.dram_tensor("out", [B_LOC, T, C], F32, kind="ExternalOutput").ap()
    wdq_d = nc.dram_tensor("wdq", [512, 1024], F32, kind="ExternalInput").ap()
    wuq_d = nc.dram_tensor("wuq", [1024, 512], F32, kind="ExternalInput").ap()
    wdkv_d = nc.dram_tensor("wdkv", [512, 1024], F32, kind="ExternalInput").ap()
    wuk_d = nc.dram_tensor("wuk", [1024, 512], F32, kind="ExternalInput").ap()
    wuv_d = nc.dram_tensor("wuv", [1024, 512], F32, kind="ExternalInput").ap()
    wo_d = nc.dram_tensor("wo", [1024, 1024], F32, kind="ExternalInput").ap()
    wr_d = nc.dram_tensor("wr", [1024, 1024], F32, kind="ExternalInput").ap()
    wk_d = nc.dram_tensor("wk", [1024, 1024], F32, kind="ExternalInput").ap()
    wv_d = nc.dram_tensor("wv", [1024, 1024], F32, kind="ExternalInput").ap()
    mur_d = nc.dram_tensor("mur", [1024], F32, kind="ExternalInput").ap()
    muk_d = nc.dram_tensor("muk", [1024], F32, kind="ExternalInput").ap()
    dbg_d = None
    if DEBUG_DUMP:
        dbg_d = nc.dram_tensor("dbg", [128, 4096], BF16, kind="ExternalOutput").ap()
        dbgo_d = nc.dram_tensor("dbgo", [128, 4096], F32, kind="ExternalOutput").ap()
        dbgr_d = nc.dram_tensor("dbgr", [128, 512], F32, kind="ExternalOutput").ap()
        dbgy_d = nc.dram_tensor("dbgy", [128, 2048], BF16, kind="ExternalOutput").ap()

    prog = Prog()

    def natural_ap(dram, rows, cols):
        return bass.AP(dram.tensor, 0, [[cols, 128], [128 * cols, rows // 128], [1, cols]])

    with ExitStack() as es:
        ec = es.enter_context
        ident = ec(nc.sbuf_tensor("ident", [128, 128], BF16))
        negmask = ec(nc.sbuf_tensor("negmask", [128, 128], BF16))
        ones_col = ec(nc.sbuf_tensor("ones_col", [128, 1], BF16))
        ones_row = ec(nc.sbuf_tensor("ones_row", [1, 128], F32))
        mu_r4 = ec(nc.sbuf_tensor("mu_r4", [128, 8], F32))
        mu_k4 = ec(nc.sbuf_tensor("mu_k4", [128, 8], F32))
        mu1m_r4 = ec(nc.sbuf_tensor("mu1m_r4", [128, 8], F32))
        mu1m_k4 = ec(nc.sbuf_tensor("mu1m_k4", [128, 8], F32))
        wdkvT = ec(nc.sbuf_tensor("wdkvT", [128, 8 * 512], BF16))
        m1 = ec(nc.sbuf_tensor("m1", [128, 8 * 1024], BF16))
        keff = ec(nc.sbuf_tensor("keff", [128, 8 * 512], BF16))
        vo = ec(nc.sbuf_tensor("vo", [128, 4 * 1024], BF16))
        wrT = ec(nc.sbuf_tensor("wrT", [128, 8 * 1024], BF16))
        wkT = ec(nc.sbuf_tensor("wkT", [128, 8 * 1024], BF16))
        wvT = ec(nc.sbuf_tensor("wvT", [128, 8 * 1024], BF16))

        bufs = {}
        def B(name):
            if name not in bufs:
                bufs[name] = Buf()
            return bufs[name]

        with ExitStack() as ps:
            pc = ps.enter_context
            stg_f = pc(nc.sbuf_tensor("stg_f", [128, 8192], F32))
            stg_b = pc(nc.sbuf_tensor("stg_b", [128, 8192], BF16))
            wuqT = pc(nc.sbuf_tensor("wuqT", [128, 4 * 1024], BF16))
            wdq_st = pc(nc.sbuf_tensor("wdq_st", [128, 4 * 1024], BF16))
            m1T = pc(nc.sbuf_tensor("m1T", [128, 8 * 1024], BF16))
            wuk_st = pc(nc.sbuf_tensor("wuk_st", [128, 8 * 512], BF16))
            wuv_st = pc(nc.sbuf_tensor("wuv_st", [128, 8 * 512], BF16))
            woT = pc(nc.sbuf_tensor("woT", [128, 8 * 1024], BF16))
            pp0 = pc(nc.psum_tensor("pp0", [128, 512], F32))
            pp1 = pc(nc.psum_tensor("pp1", [128, 512], F32))
            ppt = pc(nc.psum_tensor("ppt", [128, 512], BF16))
            ppt2 = pc(nc.psum_tensor("ppt2", [128, 512], BF16))
            pps = [pp0, pp1]
            ppb = [B("pp0"), B("pp1")]

            def P(eng, fn, reads=(), writes=(), dma=False, accum=False):
                return prog.op(eng, fn, reads, writes, dma=dma, accum=accum, phase=0)

            P("gpsimd", lambda q: q.iota(stg_f[:, 0:128], [[1, 128]],
                                         channel_multiplier=-1,
                                         allow_small_or_imprecise_dtypes=True),
              writes=[B("stg_f0")])
            P("vector", lambda q: q.tensor_single_scalar(
                stg_f[:, 128:256], stg_f[:, 0:128], 0.0, ALU.is_equal),
              reads=[B("stg_f0")], writes=[B("stg_f0")])
            P("scalar", lambda q: q.activation(ident[:], stg_f[:, 128:256], AF.Copy),
              reads=[B("stg_f0")], writes=[B("ident")])
            P("vector", lambda q: q.tensor_single_scalar(
                negmask[:], stg_f[:, 0:128], 0.0, ALU.is_ge),
              reads=[B("stg_f0")], writes=[B("negmask")])
            P("vector", lambda q: q.memset(ones_col[:], 1.0), writes=[B("ones_col")])
            P("vector", lambda q: q.memset(ones_row[:], 1.0), writes=[B("ones_row")])
            mur_ap = bass.AP(mur_d.tensor, 0, [[1, 128], [128, 8]])
            muk_ap = bass.AP(muk_d.tensor, 0, [[1, 128], [128, 8]])
            P("sync", lambda q: q.dma_start(mu_r4[:], mur_ap,
                                            allow_slow_non_contiguous=True),
              writes=[B("mu_r4")], dma=True)
            P("sync", lambda q: q.dma_start(mu_k4[:], muk_ap,
                                            allow_slow_non_contiguous=True),
              writes=[B("mu_k4")], dma=True)
            P("vector", lambda q: q.tensor_scalar(mu1m_r4[:], mu_r4[:], -1.0, 1.0,
                                                  ALU.mult, ALU.add),
              reads=[B("mu_r4")], writes=[B("mu1m_r4")])
            P("vector", lambda q: q.tensor_scalar(mu1m_k4[:], mu_k4[:], -1.0, 1.0,
                                                  ALU.mult, ALU.add),
              reads=[B("mu_k4")], writes=[B("mu1m_k4")])

            stghalf = [0]
            stgfb = [B("stg_f0"), B("stg_f1")]
            stgbb = [B("stg_b0"), B("stg_b1")]
            ppts = [ppt, ppt2]
            pptb = [B("ppt0"), B("ppt1")]

            def load_half(dram, row0, nchunk, cols):
                h2 = stghalf[0] % 2
                stghalf[0] += 1
                src = bass.AP(dram.tensor, row0 * cols,
                              [[cols, 128], [128 * cols, nchunk], [1, cols]])
                dst = bass.AP(stg_f, h2 * 4096, [[8192, 128], [cols, nchunk], [1, cols]])
                P("sync" if h2 == 0 else "scalar",
                  lambda q, s=src, d=dst: q.dma_start(d, s),
                  writes=[stgfb[h2]], dma=True)
                return h2, stg_f[:, h2 * 4096: h2 * 4096 + nchunk * cols]

            def load_convert(dram, rows, dst_ap, dst_buf):
                cols = dram.shape[1]
                nchunk = rows // 128
                assert nchunk * cols <= 4096
                h2, view_f = load_half(dram, 0, nchunk, cols)
                half_elems = nchunk * cols // 2
                tok = B(f"cvt_tok_{dram.name}")
                P("scalar", lambda q, a=dst_ap[:, 0:half_elems],
                  b=view_f[:, 0:half_elems]: q.activation(a, b, AF.Copy),
                  reads=[stgfb[h2]], writes=[dst_buf, tok])
                P("vector", lambda q, a=dst_ap[:, half_elems:2 * half_elems],
                  b=view_f[:, half_elems:2 * half_elems]: q.tensor_copy(a, b),
                  reads=[stgfb[h2], tok], writes=[dst_buf])

            def load_transpose(dram, rows, chW, dstT, dstT_buf, dst_cw):
                cols = chW
                nr = rows // 128
                nch = cols // 128
                nhalves = max((nr * cols + 4095) // 4096, (nr + 3) // 4)
                rows_per_half = nr // nhalves
                for hh in range(nhalves):
                    h2, view_f = load_half(dram, hh * rows_per_half * 128,
                                           rows_per_half, cols)
                    nn = rows_per_half * cols
                    vb1 = stg_b[:, h2 * 4096: h2 * 4096 + nn // 2]
                    vb2 = stg_b[:, h2 * 4096 + nn // 2: h2 * 4096 + nn]
                    tok = B(f"cvt_tok_b{h2}")
                    P("scalar", lambda q, a=vb1, b=view_f[:, 0:nn // 2]:
                      q.activation(a, b, AF.Copy),
                      reads=[stgfb[h2]], writes=[stgbb[h2], tok])
                    P("vector", lambda q, a=vb2, b=view_f[:, nn // 2:nn]:
                      q.tensor_copy(a, b),
                      reads=[stgfb[h2], tok], writes=[stgbb[h2]])
                    for ci in range(nch):
                        pt = ci % 2
                        for rj in range(rows_per_half):
                            src_t = stg_b[:, h2 * 4096 + rj * cols + ci * 128:
                                          h2 * 4096 + rj * cols + ci * 128 + 128]
                            dst_t = ppts[pt][:, rj * 128: rj * 128 + 128]
                            P("tensor",
                              lambda q, a=dst_t, b=src_t: q.transpose(a, b, ident[:]),
                              reads=[stgbb[h2], B("ident")], writes=[pptb[pt]],
                              accum=(rj > 0))
                        dst_sl = dstT[:, ci * dst_cw + hh * rows_per_half * 128:
                                      ci * dst_cw + (hh + 1) * rows_per_half * 128]
                        src_sl = ppts[pt][:, 0:rows_per_half * 128]
                        P("vector", lambda q, a=dst_sl, b=src_sl: q.tensor_copy(a, b),
                          reads=[pptb[pt]], writes=[dstT_buf])

            pi = 0
            def prep_mm(dst_ap, dst_buf, lhs_fn, rhs_fn, nk, reads, scale=None):
                nonlocal pi
                ps_t, ps_b = pps[pi % 2], ppb[pi % 2]
                pi += 1
                for k in range(nk):
                    P("tensor", lambda q, o=ps_t[:], l=lhs_fn(k), r=rhs_fn(k),
                      s1=(k == 0), s2=(k == nk - 1): q.matmul(o, l, r, start=s1, stop=s2),
                      reads=reads, writes=[ps_b], accum=(k > 0))
                if scale is None:
                    P("scalar", lambda q, a=dst_ap, b=ps_t[:]: q.activation(a, b, AF.Copy),
                      reads=[ps_b], writes=[dst_buf])
                else:
                    P("scalar", lambda q, a=dst_ap, b=ps_t[:], sc=scale:
                      q.activation(a, b, AF.Copy, scale=sc),
                      reads=[ps_b], writes=[dst_buf])

            load_transpose(wuq_d, 1024, 512, wuqT, B("wuqT"), 1024)
            load_convert(wdq_d, 512, wdq_st[:, 0:4 * 1024], B("wdq_st"))
            for ci in range(8):
                for h2 in range(2):
                    prep_mm(m1[:, ci * 1024 + h2 * 512: ci * 1024 + h2 * 512 + 512],
                            B("m1"),
                            lambda k, ci=ci: wdq_st[:, k * 1024 + ci * 128: k * 1024 + ci * 128 + 128],
                            lambda k, h2=h2: wuqT[:, k * 1024 + h2 * 512: k * 1024 + h2 * 512 + 512],
                            4, [B("wdq_st"), B("wuqT")])
            for ci in range(8):
                for h2 in range(2):
                    prep_mm(m1T[:, ci * 1024 + h2 * 512: ci * 1024 + h2 * 512 + 512],
                            B("m1T"),
                            lambda k, ci=ci: wuqT[:, k * 1024 + ci * 128: k * 1024 + ci * 128 + 128],
                            lambda k, h2=h2: wdq_st[:, k * 1024 + h2 * 512: k * 1024 + h2 * 512 + 512],
                            4, [B("wdq_st"), B("wuqT")])
            load_convert(wuk_d, 1024, wuk_st[:, 0:8 * 512], B("wuk_st"))
            for ri in range(8):
                prep_mm(keff[:, ri * 512: ri * 512 + 512], B("keff"),
                        lambda k, ri=ri: m1T[:, k * 1024 + ri * 128: k * 1024 + ri * 128 + 128],
                        lambda k: wuk_st[:, k * 512: k * 512 + 512],
                        8, [B("m1T"), B("wuk_st")], scale=0.125)
            load_convert(wuv_d, 1024, wuv_st[:, 0:8 * 512], B("wuv_st"))
            load_transpose(wdkv_d, 512, 1024, wdkvT, B("wdkvT"), 512)
            load_transpose(wo_d, 1024, 1024, woT, B("woT"), 1024)
            load_transpose(wr_d, 1024, 1024, wrT, B("wrT"), 1024)
            load_transpose(wk_d, 1024, 1024, wkT, B("wkT"), 1024)
            load_transpose(wv_d, 1024, 1024, wvT, B("wvT"), 1024)
            for li in range(4):
                for h2 in range(2):
                    prep_mm(vo[:, li * 1024 + h2 * 512: li * 1024 + h2 * 512 + 512],
                            B("vo"),
                            lambda k, li=li: wuv_st[:, k * 512 + li * 128: k * 512 + li * 128 + 128],
                            lambda k, h2=h2: woT[:, k * 1024 + h2 * 512: k * 1024 + h2 * 512 + 512],
                            8, [B("wuv_st"), B("woT")])

        ckvT = ec(nc.sbuf_tensor("ckvT", [128, 4 * 2048], BF16))
        ckvtok = ec(nc.sbuf_tensor("ckvtok", [128, 16 * 512], BF16))
        xnT = ec(nc.sbuf_tensor("xnT", [128, 8 * 512], BF16))
        qT = ec(nc.sbuf_tensor("qT", [128, 8 * 512], BF16))
        pcol0 = ec(nc.sbuf_tensor("pcol0", [128, 8], BF16))
        pcol1 = ec(nc.sbuf_tensor("pcol1", [128, 8], BF16))
        outacc = ec(nc.sbuf_tensor("outacc", [128, 4096], F32))
        qlat = ec(nc.sbuf_tensor("qlat", [128, 4 * 512], BF16))
        qlat1 = ec(nc.sbuf_tensor("qlat1", [128, 4 * 512], BF16))
        a_t0 = ec(nc.sbuf_tensor("a_t0", [128, 512], BF16))
        a_t1 = ec(nc.sbuf_tensor("a_t1", [128, 512], BF16))
        a_t2 = ec(nc.sbuf_tensor("a_t2", [128, 512], BF16))
        a_t3 = ec(nc.sbuf_tensor("a_t3", [128, 512], BF16))
        a_t4 = ec(nc.sbuf_tensor("a_t4", [128, 512], BF16))
        a_t5 = ec(nc.sbuf_tensor("a_t5", [128, 512], BF16))
        ylat = ec(nc.sbuf_tensor("ylat", [128, 4 * 512], BF16))
        recip = ec(nc.sbuf_tensor("recip", [128, 512], F32))
        scr512 = ec(nc.sbuf_tensor("scr512", [128, 512], F32))
        xr_g = ec(nc.sbuf_tensor("xr_g", [128, 8 * 512], BF16))
        xk_g = ec(nc.sbuf_tensor("xk_g", [128, 8 * 512], BF16))
        r_tile = recip
        bufs["r_tile"] = B("recip")
        kact = qT
        bufs["kact"] = B("qT")
        xt0 = ec(nc.sbuf_tensor("xt0", [128, 1024], F32))
        xt1 = ec(nc.sbuf_tensor("xt1", [128, 1024], F32))
        xn0 = ec(nc.sbuf_tensor("xn0", [128, 1024], BF16))
        xn1 = ec(nc.sbuf_tensor("xn1", [128, 1024], BF16))
        xn2 = ec(nc.sbuf_tensor("xn2", [128, 1024], BF16))
        xn3 = ec(nc.sbuf_tensor("xn3", [128, 1024], BF16))
        rc = ec(nc.sbuf_tensor("rc", [128, 8], F32))
        srow = recip
        bufs["srow"] = B("recip")
        stats = [[ec(nc.sbuf_tensor(f"st{n}{j}", [128, w], F32))
                  for n, w in (("bn", 12), ("ag", 2), ("rstd", 1), ("nbias", 1))]
                 for j in range(2)]
        statb = [[B(f"st{n}{j}") for n in ("bn", "ag", "rstd", "nbias")]
                 for j in range(2)]

        ps_a = ec(nc.psum_tensor("ps_a", [128, 512], F32))
        ps_b = ec(nc.psum_tensor("ps_b", [128, 512], F32))
        ps_y = [ec(nc.psum_tensor(f"ps_y{i}", [128, 512], F32)) for i in range(4)]
        ps_m = ec(nc.psum_tensor("ps_m", [128, 512], F32))
        ps_bc = ec(nc.psum_tensor("ps_bc", [128, 512], F32))

        pst = [ps_a, ps_b]
        psb = [B("ps_a"), B("ps_b")]
        pyb = [B(f"ps_y{i}") for i in range(4)]
        pst4 = [ps_a, ps_b, ps_y[0], ps_y[1]]
        psb4 = [B("ps_a"), B("ps_b"), pyb[0], pyb[1]]
        pmb = B("ps_m")
        pbcb = B("ps_bc")
        xts = [xt0, xt1]
        xtb = [B("xt0"), B("xt1")]
        xns = [xn0, xn1, xn2, xn3]
        xnb = [B("xn0"), B("xn1"), B("xn2"), B("xn3")]
        ats = [a_t0, a_t1, a_t2, a_t3, a_t4, a_t5]
        atb = [B("a_t0"), B("a_t1"), B("a_t2"), B("a_t3"), B("a_t4"), B("a_t5")]
        NAT = 6
        qlats = [qlat, qlat1]
        qlb = [[B(f"qlat{j}_{i}") for i in range(4)] for j in range(2)]
        ylb = [B(f"ylat{i}") for i in range(4)]
        ylb0s = [B(f"ylat0s{i}") for i in range(4)]
        pcols = [pcol0, pcol1]
        pcolb = [B("pcol0"), B("pcol1")]

        def M(eng, fn, reads=(), writes=(), dma=False, accum=False):
            return prog.op(eng, fn, reads, writes, dma=dma, accum=accum, phase=1)

        store_ids = []
        mm_ping = [0]

        def emit_mix_mult(g):
            xn3 = xnT[:].rearrange("p (c t) -> p c t", c=8)
            for dst, dstb, mu_t, mub in (
                    (xr_g, B("xr_g"), mu_r4, B("mu_r4")),
                    (xk_g, B("xk_g"), mu_k4, B("mu_k4"))):
                mu_bc = bass.AP(mu_t, 0, [[8, 128], [1, 8], [0, 512]])
                d3 = dst[:].rearrange("p (c t) -> p c t", c=8)
                M("gpsimd", lambda q, a=d3, b=xn3, c2=mu_bc:
                  q.tensor_tensor(a, b, c2, ALU.mult),
                  reads=[B("xnT"), mub], writes=[dstb])

        def emit_mix_chunk(g, ci):
            pc, pcb = pcols[g % 2], pcolb[g % 2]
            for dst, dstb, mu1m_t, mu1mb in (
                    (xr_g, B("xr_g"), mu1m_r4, B("mu1m_r4")),
                    (xk_g, B("xk_g"), mu1m_k4, B("mu1m_k4"))):
                o_ap = dst[:, ci * 512 + 1: ci * 512 + 512]
                i_ap = xnT[:, ci * 512: ci * 512 + 511]
                sc = mu1m_t[:, ci: ci + 1]
                M("vector", lambda q, a=o_ap, b=i_ap, s1=sc:
                  q.scalar_tensor_tensor(a, b, s1, a, ALU.mult, ALU.add),
                  reads=[B("xnT"), mu1mb, dstb], writes=[dstb])
                o0 = dst[:, ci * 512: ci * 512 + 1]
                p0_ap = pc[:, ci: ci + 1]
                M("vector", lambda q, a=o0, b=p0_ap, s1=sc:
                  q.scalar_tensor_tensor(a, b, s1, a, ALU.mult, ALU.add),
                  reads=[pcb, mu1mb, dstb], writes=[dstb])

        def pe_acc(dst_ap_fn, dst_buf, lhs_list, rhs_list, reads, nbank=2):
            ps_t, ps_bb = pst4[mm_ping[0] % nbank], psb4[mm_ping[0] % nbank]
            mm_ping[0] += 1
            n = len(lhs_list)
            for k in range(n):
                M("tensor", lambda q, o=ps_t[:], l=lhs_list[k], r=rhs_list[k],
                  s1=(k == 0), s2=(k == n - 1): q.matmul(o, l, r, start=s1, stop=s2),
                  reads=reads, writes=[ps_bb], accum=(k > 0))
            return ps_t, ps_bb


        iters = [(s, g) for s in range(B_LOC) for g in range(NGRP)]

        def emit_ln_tile(s, g, i):
            t0 = (g * NTIL + i) * 128
            xt, xb = xts[i % 2], xtb[i % 2]
            xn, xnbb = xns[i], xnb[i]
            (bn, ag, rstd, nbias) = stats[i % 2]
            (bn_b, ag_b, rstd_b, nbias_b) = statb[i % 2]
            M("sync", lambda q, d=xt[:], sl=x_d[s, t0:t0 + 128, :]:
              q.dma_start(d, sl), writes=[xb], dma=True)
            M("vector", lambda q, a=bn[:, 0:6], b=xt[:, 0:512]: q.bn_stats(a, b),
              reads=[xb], writes=[bn_b])
            M("vector", lambda q, a=bn[:, 6:12], b=xt[:, 512:1024]: q.bn_stats(a, b),
              reads=[xb], writes=[bn_b])
            M("vector", lambda q, a=ag[:], b=bn[:]: q.bn_aggr(a, b),
              reads=[bn_b], writes=[ag_b])
            M("vector", lambda q, a=ag[:, 1:2]: q.tensor_scalar_add(a, a, 1.0e-5),
              reads=[ag_b], writes=[ag_b])
            M("scalar", lambda q, a=ag[:, 1:2]: q.activation(a, a, AF.Ln),
              reads=[ag_b], writes=[ag_b])
            M("scalar", lambda q, a=rstd[:], b=ag[:, 1:2]:
              q.activation(a, b, AF.Exp, scale=-0.5),
              reads=[ag_b], writes=[rstd_b])
            M("vector", lambda q, a=nbias[:], b=ag[:, 0:1], c2=rstd[:]:
              q.scalar_tensor_tensor(a, b, -1.0, c2, ALU.mult, ALU.mult),
              reads=[ag_b, rstd_b], writes=[nbias_b])
            nbias_bc = bass.AP(nbias, 0, [[1, 128], [0, 1024]])
            M("vector", lambda q, a=xn[:], b=xt[:], nb=nbias_bc, sc=rstd[:]:
              q.scalar_tensor_tensor(a, b, sc, nb, ALU.mult, ALU.add),
              reads=[xb, nbias_b, rstd_b], writes=[xnbb])

        def emit_A(s, g):
            ckvT_b = B(f"ckvT_g{g}")
            ckvtok_b = B(f"ckvtok_g{g}")
            for i in range(NTIL):
                xn, xnbb = xns[i], xnb[i]
                ps_t, ps_bb = pst[i % 2], psb[i % 2]
                ps_v = ps_t[:].bitcast(BF16)
                for ci in range(8):
                    M("tensor", lambda q, a=ps_v[:, ci * 128:(ci + 1) * 128],
                      b=xn[:, ci * 128:(ci + 1) * 128]:
                      q.transpose(a, b, ident[:]),
                      reads=[xnbb, B("ident")], writes=[ps_bb],
                      accum=(ci > 0))
                src = ps_v.rearrange("p (c t) -> p c t", c=8)
                dst = bass.AP(xnT, i * 128, [[4096, 128], [512, 8], [1, 128]])
                M("vector", lambda q, a=dst, b=src: q.tensor_copy(a, b),
                  reads=[ps_bb], writes=[B("xnT")])
            if g < NGRP - 1:
                src = bass.AP(xnT, 511, [[4096, 128], [512, 8], [1, 1]])
                dst = bass.AP(pcols[(g + 1) % 2], 0, [[8, 128], [1, 8], [1, 1]])
                M("vector", lambda q, a=dst, b=src: q.tensor_copy(a, b),
                  reads=[B("xnT")], writes=[pcolb[(g + 1) % 2]])
            for lc in range(4):
                ps_t, ps_bb = pe_acc(None, None,
                    [wdkvT[:, ci * 512 + lc * 128: ci * 512 + lc * 128 + 128] for ci in range(8)],
                    [xnT[:, ci * 512: ci * 512 + 512] for ci in range(8)],
                    [B("wdkvT"), B("xnT")], nbank=4)
                dst = ckvT[:, lc * 2048 + g * 512: lc * 2048 + g * 512 + 512]
                M("scalar", lambda q, a=dst, b=ps_t[:]: q.activation(a, b, AF.Copy),
                  reads=[ps_bb], writes=[ckvT_b])
            for tt in range(NTIL):
                ps_t, ps_bb = pe_acc(None, None,
                    [xnT[:, ci * 512 + tt * 128: ci * 512 + tt * 128 + 128] for ci in range(8)],
                    [wdkvT[:, ci * 512: ci * 512 + 512] for ci in range(8)],
                    [B("wdkvT"), B("xnT")], nbank=4)
                dst = ckvtok[:, (g * NTIL + tt) * 512: (g * NTIL + tt) * 512 + 512]
                M("scalar", lambda q, a=dst, b=ps_t[:]: q.activation(a, b, AF.Copy),
                  reads=[ps_bb], writes=[ckvtok_b])
            for cqc in range(8):
                ps_t, ps_bb = pe_acc(None, None,
                    [m1[:, ci * 1024 + cqc * 128: ci * 1024 + cqc * 128 + 128] for ci in range(8)],
                    [xnT[:, ci * 512: ci * 512 + 512] for ci in range(8)],
                    [B("m1"), B("xnT")], nbank=4)
                dst = qT[:, cqc * 512: cqc * 512 + 512]
                M("vector", lambda q, a=dst, b=ps_t[:]: q.tensor_copy(a, b),
                  reads=[ps_bb], writes=[B("qT")])

        srow_ap = recip[0:1, 0:512]

        def emit_bc_rows(h):
            M("scalar", lambda q: q.activation(srow_ap, ps_m[0:1, 0:512], AF.Ln),
              reads=[pmb], writes=[B("recip")])
            M("scalar", lambda q: q.activation(srow_ap, srow_ap, AF.Exp, scale=-1.0),
              reads=[B("recip")], writes=[B("recip")])

        def emit_bc_rest(h):
            for sub in range(4):
                M("tensor", lambda q, o=ps_bc[:, sub:sub + 1],
                  l=srow_ap[:, sub * 128:(sub + 1) * 128],
                  s1=(sub == 0), s2=(sub == 3):
                  q.matmul(o, l, ones_row[0:1, 0:1], start=s1, stop=s2),
                  reads=[B("ones_row"), B("recip")], writes=[pbcb],
                  accum=(sub > 0))
            rcd = rc[:, (h % 2) * 4:(h % 2) * 4 + 4]
            M("vector", lambda q, a=rcd, b=ps_bc[:, 0:4]: q.tensor_copy(a, b),
              reads=[pbcb], writes=[B("rc")])
            for c4 in range(4):
                M("vector", lambda q, a=ylat[:, c4 * 128:(c4 + 1) * 128],
                  b=ps_y[0][:, c4 * 128:(c4 + 1) * 128]:
                  q.tensor_copy(a, b),
                  reads=[pyb[0]], writes=[ylb0s[c4]])
            for lc in range(1, 4):
                M("vector", lambda q, a=ylat[:, lc * 512:(lc + 1) * 512],
                  b=ps_y[lc][:]: q.tensor_copy(a, b),
                  reads=[pyb[lc]], writes=[ylb[lc]])

        def emit_y_lc(h, lc):
            for sub in range(4):
                rb = ylb0s[sub] if lc == 0 else ylb[lc]
                M("tensor", lambda q,
                  o=ps_m[:, sub * 64: sub * 64 + 64],
                  l=ylat[:, lc * 512 + sub * 128: lc * 512 + sub * 128 + 128],
                  r=vo[:, lc * 1024 + h * 64: lc * 1024 + h * 64 + 64],
                  s1=(lc == 0 and sub == 0), s2=(lc == 3 and sub == 3):
                  q.matmul(o, l, r, start=s1, stop=s2),
                  reads=[rb, B("vo")], writes=[pmb], accum=(lc > 0 or sub > 0))

        def emit_y_evac(h):
            ysrc = bass.AP(ps_m, 0, [[512, 128], [64, 4], [1, 64]])
            ydst = bass.AP(outacc, h * 64, [[4096, 128], [1024, 4], [1, 64]])
            rc_bc = bass.AP(rc, (h % 2) * 4, [[8, 128], [1, 4], [0, 64]])
            M("vector", lambda q, a=ydst, b=ysrc, c2=rc_bc:
              q.tensor_tensor(a, b, c2, ALU.mult),
              reads=[pmb, B("rc")], writes=[B("outacc")])

        def emit_y(h):
            for lc in range(4):
                emit_y_lc(h, lc)
            emit_y_evac(h)

        def emit_k_ffn(g):
            for dkc in range(8):
                ps_t, ps_bb = pe_acc(None, None,
                    [wkT[:, ci * 1024 + dkc * 128: ci * 1024 + dkc * 128 + 128] for ci in range(8)],
                    [xk_g[:, ci * 512: ci * 512 + 512] for ci in range(8)],
                    [B("wkT"), B("xk_g")], nbank=4)
                M("scalar", lambda q, a=scr512[:], b=ps_t[:]: q.activation(a, b, AF.Relu),
                  reads=[ps_bb], writes=[B("scr512")])
                M("scalar", lambda q, a=kact[:, dkc * 512:(dkc + 1) * 512], b=scr512[:]:
                  q.activation(a, b, AF.Square),
                  reads=[B("scr512")], writes=[B("kact")])

        def emit_B(s, g, early_ln):
            K = 4 * g + 4
            LEAD = 2
            pend = [None]

            def scores(h, kt):
                q0 = max(0, (kt - 4 * g) * 128)
                ps_t, ps_bb = pst[kt % 2], psb[kt % 2]
                ckv_rb = B(f"ckvT_g{kt // 4}")
                ql_t = qlats[h % 2]
                for lc in range(4):
                    lhs = ckvT[:, lc * 2048 + kt * 128: lc * 2048 + kt * 128 + 128]
                    rhs = ql_t[:, lc * 512 + q0: lc * 512 + 512]
                    M("tensor", lambda q, o=ps_t[:, q0:512], l=lhs, r=rhs,
                      s1=(lc == 0), s2=(lc == 3):
                      q.matmul(o, l, r, start=s1, stop=s2),
                      reads=[ckv_rb, qlb[h % 2][lc]], writes=[ps_bb], accum=(lc > 0))

            def qlat_stage(hh, lc, bank=None, evac="vector"):
                p0 = (hh % 2) * 64
                hc = hh // 2
                ps_t, ps_bb = (ps_bc, pbcb) if bank is None else bank
                lhs = keff[p0:p0 + 64, hc * 512 + lc * 128: hc * 512 + lc * 128 + 128]
                rhs = qT[p0:p0 + 64, hc * 512: hc * 512 + 512]
                M("tensor", lambda q, o=ps_t[:], l=lhs, r=rhs:
                  q.matmul(o, l, r, start=True, stop=True),
                  reads=[B("keff"), B("qT")], writes=[ps_bb])
                ql_t = qlats[hh % 2]
                if evac == "vector":
                    M("vector", lambda q, a=ql_t[:, lc * 512:(lc + 1) * 512],
                      b=ps_t[:]: q.tensor_copy(a, b),
                      reads=[ps_bb], writes=[qlb[hh % 2][lc]])
                else:
                    M("scalar", lambda q, a=ql_t[:, lc * 512:(lc + 1) * 512],
                      b=ps_t[:]: q.activation(a, b, AF.Copy),
                      reads=[ps_bb], writes=[qlb[hh % 2][lc]])

            def maskexp(kt):
                q0 = max(0, (kt - 4 * g) * 128)
                ps_t, ps_bb = pst[kt % 2], psb[kt % 2]
                at, ab = ats[kt % NAT], atb[kt % NAT]
                M("scalar", lambda q, a=at[:, q0:512], b=ps_t[:, q0:512]:
                  q.activation(a, b, AF.Exp),
                  reads=[ps_bb], writes=[ab])
                if kt >= 4 * g:
                    M("vector", lambda q, a=at[:, q0:q0 + 128]:
                      q.tensor_tensor(a, a, negmask[:], ALU.mult),
                      reads=[ab, B("negmask")], writes=[ab])

            def ylat_pair(kts):
                for lc in range(4):
                    for kt in kts:
                        q0 = max(0, (kt - 4 * g) * 128)
                        at, ab = ats[kt % NAT], atb[kt % NAT]
                        ctok_rb = B(f"ckvtok_g{kt // 4}")
                        lhs = ckvtok[:, kt * 512 + lc * 128: kt * 512 + lc * 128 + 128]
                        M("tensor", lambda q, o=ps_y[lc][:, q0:512], l=lhs,
                          r=at[:, q0:512], s1=(kt == 0), s2=(kt == K - 1):
                          q.matmul(o, l, r, start=s1, stop=s2),
                          reads=[ctok_rb, ab], writes=[pyb[lc]], accum=(kt > 0))

            def sums_only(kt):
                q0 = max(0, (kt - 4 * g) * 128)
                at, ab = ats[kt % NAT], atb[kt % NAT]
                M("tensor", lambda q, o=ps_m[0:1, q0:512], l=ones_col[:],
                  r=at[:, q0:512], s1=(kt == 0), s2=(kt == K - 1):
                  q.matmul(o, l, r, start=s1, stop=s2),
                  reads=[B("ones_col"), ab], writes=[pmb], accum=(kt > 0))


            for h in range(NH):
                if h == 0:
                    for lc in range(4):
                        qlat_stage(0, lc)
                for hh, fn in early_ln:
                    if hh == h:
                        fn()
                scores(h, 0)
                maskexp(0)
                scores(h, 1)
                maskexp(1)
                if pend[0] is not None:
                    emit_bc_rest(pend[0])
                scores(h, 2)
                maskexp(2)
                scores(h, 3)
                maskexp(3)
                if K == 4 and h + 1 < NH:
                    g0banks = [None, (ps_m, pmb), (ps_a, psb[0]), (ps_b, psb[1])]
                    for lc in range(4):
                        qlat_stage(h + 1, lc, bank=g0banks[lc],
                                   evac="scalar" if lc % 2 == 0 else "vector")
                if pend[0] is not None:
                    emit_y(pend[0])
                    pend[0] = None
                nyl = [0]

                def after_ylat():
                    if K > 4 and h + 1 < NH and nyl[0] < 4:
                        qlat_stage(h + 1, nyl[0])
                    nyl[0] += 1
                LEAD = 5 if K > 4 else 4
                if K > 4:
                    scores(h, 4)
                    maskexp(4)
                ypend = []

                def ylat_flush():
                    if ypend:
                        ylat_pair(tuple(ypend))
                        for _ in ypend:
                            after_ylat()
                        ypend.clear()
                for kt in range(LEAD, K):
                    scores(h, kt)
                    maskexp(kt)
                    sums_only(kt - LEAD)
                    ypend.append(kt - LEAD)
                    if len(ypend) == 2:
                        ylat_flush()
                ylat_flush()
                for kt in range(max(0, K - LEAD), K):
                    sums_only(kt)
                emit_bc_rows(h)
                for kt in range(max(0, K - LEAD), K):
                    ypend.append(kt)
                    if len(ypend) == 2:
                        ylat_flush()
                ylat_flush()
                pend[0] = h
            emit_bc_rest(pend[0])
            emit_k_ffn(g)
            emit_y(pend[0])
            pend[0] = None

        def emit_C(s, g):
            for tt in range(NTIL):
                t0 = (g * NTIL + tt) * 128
                xt, xb = xts[tt % 2], xtb[tt % 2]
                M("sync", lambda q, d=xt[:], sl=x_d[s, t0:t0 + 128, :]:
                  q.dma_start(d, sl), writes=[xb], dma=True)
                for dh in range(2):
                    ps_t, ps_bb = pe_acc(None, None,
                        [xr_g[:, ci * 512 + tt * 128: ci * 512 + tt * 128 + 128] for ci in range(8)],
                        [wrT[:, ci * 1024 + dh * 512: ci * 1024 + dh * 512 + 512] for ci in range(8)],
                        [B("wrT"), B("xr_g")], nbank=4)
                    rh = r_tile[:, 0:512]
                    M("scalar", lambda q, a=rh, b=ps_t[:]:
                      q.activation(a, b, AF.Exp, scale=-1.0),
                      reads=[ps_bb], writes=[B("r_tile")])
                    M("vector", lambda q, a=rh: q.tensor_scalar_add(a, a, 1.0),
                      reads=[B("r_tile")], writes=[B("r_tile")])
                    M("scalar", lambda q, a=rh: q.activation(a, a, AF.Ln),
                      reads=[B("r_tile")], writes=[B("r_tile")])
                    M("scalar", lambda q, a=rh: q.activation(a, a, AF.Exp, scale=-1.0),
                      reads=[B("r_tile")], writes=[B("r_tile")])
                    ps_t, ps_bb = pe_acc(None, None,
                        [kact[:, dkc * 512 + tt * 128: dkc * 512 + tt * 128 + 128] for dkc in range(8)],
                        [wvT[:, dkc * 1024 + dh * 512: dkc * 1024 + dh * 512 + 512] for dkc in range(8)],
                        [B("wvT"), B("kact")], nbank=4)
                    M("vector", lambda q, a=scr512[:], b=rh, c2=ps_t[:]:
                      q.tensor_tensor(a, b, c2, ALU.mult),
                      reads=[B("r_tile"), ps_bb], writes=[B("scr512")])
                    oa = outacc[:, tt * 1024 + dh * 512: tt * 1024 + dh * 512 + 512]
                    M("vector", lambda q, a=oa: q.tensor_tensor(a, a, scr512[:], ALU.add),
                      reads=[B("scr512"), B("outacc")], writes=[B("outacc")])
                oa = outacc[:, tt * 1024: tt * 1024 + 1024]
                M("vector", lambda q, a=oa, b=xt[:]: q.tensor_tensor(a, a, b, ALU.add),
                  reads=[xb, B("outacc")], writes=[B("outacc")])
            g0t = g * TG
            dsts = bass.AP(out_d.tensor, (s * T + g0t) * 1024,
                           [[1024, 128], [128 * 1024, 4], [1, 1024]])
            srcs = outacc[:].rearrange("p (c t) -> p c t", c=4)
            sid = M("scalar", lambda q, d=dsts, b=srcs: q.dma_start(d, b),
                    reads=[B("outacc")], dma=True)
            store_ids.append(sid)

        for i in range(NTIL):
            emit_ln_tile(iters[0][0], iters[0][1], i)
        for it, (s, g) in enumerate(iters):
            if g == 0:
                M("vector", lambda q: q.memset(pcol0[:], 0.0), writes=[pcolb[0]])
            emit_A(s, g)
            if DEBUG_DUMP and it == 0:
                sid = M("scalar", lambda q: q.dma_start(dbg_d[:, :], xnT[:]),
                        reads=[B("xnT")], dma=True)
                store_ids.append(sid)
            early = [(1, lambda gg=g: emit_mix_mult(gg))]
            for ci in range(8):
                early.append((2 + ci, lambda gg=g, cc=ci: emit_mix_chunk(gg, cc)))
            if it + 1 < len(iters):
                s2, g2 = iters[it + 1]
                for i in range(NTIL):
                    early.append((6 + 2 * i,
                                  lambda ss=s2, gg=g2, ii=i: emit_ln_tile(ss, gg, ii)))
            emit_B(s, g, early)
            if DEBUG_DUMP and it == 0:
                sid = M("scalar", lambda q: q.dma_start(dbgo_d[:, :], outacc[:]),
                        reads=[B("outacc")], dma=True)
                store_ids.append(sid)
                sid = M("scalar", lambda q: q.dma_start(dbgr_d[:, :], recip[:]),
                        reads=[B("recip")], dma=True)
                store_ids.append(sid)
                sid = M("scalar", lambda q: q.dma_start(dbgy_d[:, :], ylat[:]),
                        reads=[ylb[0], ylb[1], ylb[2], ylb[3]], dma=True)
                store_ids.append(sid)
            emit_C(s, g)

        prog.finalize(extra_targets=store_ids)

        sems = {}
        for key in [("tensor", False), ("vector", False), ("scalar", False),
                    ("gpsimd", False), ("sync", True), ("scalar", True),
                    ("sync", False)]:
            sems[key] = ec(nc.semaphore(f"sem_{key[0]}_{'d' if key[1] else 'c'}"))

        with nc.Block() as b0:
            for eng in Prog.ENGINES:
                getattr(b0, eng)(prog.runner(eng, 0, sems))
        with nc.Block() as b1:
            for eng in Prog.ENGINES:
                getattr(b1, eng)(prog.runner(eng, 1, sems))
        sc_total = prog.sem_totals[("scalar", True)]
        with nc.Block() as b2:
            @b2.scalar
            def _(q):
                q.wait_ge(sems[("scalar", True)], sc_total)

    return nc


def kernel(**inputs):
    x = np.ascontiguousarray(inputs["L0_attn_x"], dtype=np.float32)
    nc = build_nc()
    weights = dict(
        wdq=np.ascontiguousarray(inputs["L0_attn_W_dq"], dtype=np.float32),
        wuq=np.ascontiguousarray(inputs["L0_attn_W_uq"], dtype=np.float32),
        wdkv=np.ascontiguousarray(inputs["L0_attn_W_dkv"], dtype=np.float32),
        wuk=np.ascontiguousarray(inputs["L0_attn_W_uk"], dtype=np.float32),
        wuv=np.ascontiguousarray(inputs["L0_attn_W_uv"], dtype=np.float32),
        wo=np.ascontiguousarray(inputs["L0_attn_W_o"], dtype=np.float32),
        wr=np.ascontiguousarray(inputs["L0_ffn_W_r"], dtype=np.float32),
        wk=np.ascontiguousarray(inputs["L0_ffn_W_k"], dtype=np.float32),
        wv=np.ascontiguousarray(inputs["L0_ffn_W_v"], dtype=np.float32),
        mur=np.ascontiguousarray(inputs["L0_ffn_mu_r"], dtype=np.float32),
        muk=np.ascontiguousarray(inputs["L0_ffn_mu_k"], dtype=np.float32),
    )
    in_maps = []
    for c in range(8):
        m = dict(weights)
        m["x"] = x[c * B_LOC:(c + 1) * B_LOC]
        in_maps.append(m)
    res = run_bass_kernel_spmd(nc, in_maps, core_ids=list(range(8)))
    return np.concatenate([r["out"] for r in res.results], axis=0)


# version 50
# speedup vs baseline: 1.0917x; 1.0917x over previous
import numpy as np
import concourse.bass as bass
import concourse.mybir as mybir
from concourse.bass_utils import run_bass_kernel_spmd
from contextlib import ExitStack

F32 = mybir.dt.float32
BF16 = mybir.dt.bfloat16
AF = mybir.ActivationFunctionType
ALU = mybir.AluOpType
AX = mybir.AxisListType

B_LOC = 4
DEBUG_DUMP = False
T = 2048
C = 1024
NH = 16
HS = 64
LAT = 512
TG = 512
NGRP = T // TG
NTIL = TG // 128


class Buf:
    __slots__ = ("last_write", "readers")

    def __init__(self):
        self.last_write = None
        self.readers = []


class Prog:
    ENGINES = ("tensor", "vector", "scalar", "sync", "gpsimd")

    def __init__(self):
        self.by_eng = {e: [] for e in self.ENGINES}
        self.nid = 0
        self.semval = {}
        self.maxw = {e: {} for e in self.ENGINES}

    def op(self, eng, fn, reads=(), writes=(), dma=False, accum=False, phase=0):
        deps = set()
        for b in reads:
            if b.last_write is not None:
                deps.add(b.last_write)
        if not accum:
            for b in writes:
                if b.last_write is not None:
                    deps.add(b.last_write)
                deps.update(b.readers)
        i = self.nid
        self.nid += 1
        self.by_eng[eng].append(
            dict(id=i, fn=fn, deps=deps, dma=dma, phase=phase))
        for b in writes:
            b.last_write = i
            if not accum:
                b.readers = []
        for b in reads:
            b.readers.append(i)
        return i

    def finalize(self, extra_targets=()):
        targets = set(extra_targets)
        for recs in self.by_eng.values():
            for r in recs:
                targets |= r["deps"]
        self.sem_totals = {}
        for eng, recs in self.by_eng.items():
            counts = {False: 0, True: 0}
            for r in recs:
                if r["id"] in targets:
                    counts[r["dma"]] += 16 if r["dma"] else 1
                    self.semval[r["id"]] = ((eng, r["dma"]), counts[r["dma"]])
            self.sem_totals[(eng, False)] = counts[False]
            self.sem_totals[(eng, True)] = counts[True]

    def runner(self, eng, phase, sems):
        recs = [r for r in self.by_eng[eng] if r["phase"] == phase]
        semval = self.semval
        maxw = self.maxw[eng]

        def run(q):
            for r in recs:
                for d in sorted(r["deps"]):
                    sv = semval.get(d)
                    if sv is None:
                        continue
                    k, v = sv
                    if maxw.get(k, 0) < v:
                        maxw[k] = v
                        q.wait_ge(sems[k], v)
                ins = r["fn"](q)
                sv = semval.get(r["id"])
                if sv is not None and ins is not None:
                    k, _ = sv
                    ins.then_inc(sems[k], 16 if r["dma"] else 1)
        return run


def build_nc():
    nc = bass.Bass("TRN2", target_bir_lowering=False)

    x_d = nc.dram_tensor("x", [B_LOC, T, C], F32, kind="ExternalInput").ap()
    out_d = nc.dram_tensor("out", [B_LOC, T, C], F32, kind="ExternalOutput").ap()
    wdq_d = nc.dram_tensor("wdq", [512, 1024], F32, kind="ExternalInput").ap()
    wuq_d = nc.dram_tensor("wuq", [1024, 512], F32, kind="ExternalInput").ap()
    wdkv_d = nc.dram_tensor("wdkv", [512, 1024], F32, kind="ExternalInput").ap()
    wuk_d = nc.dram_tensor("wuk", [1024, 512], F32, kind="ExternalInput").ap()
    wuv_d = nc.dram_tensor("wuv", [1024, 512], F32, kind="ExternalInput").ap()
    wo_d = nc.dram_tensor("wo", [1024, 1024], F32, kind="ExternalInput").ap()
    wr_d = nc.dram_tensor("wr", [1024, 1024], F32, kind="ExternalInput").ap()
    wk_d = nc.dram_tensor("wk", [1024, 1024], F32, kind="ExternalInput").ap()
    wv_d = nc.dram_tensor("wv", [1024, 1024], F32, kind="ExternalInput").ap()
    mur_d = nc.dram_tensor("mur", [1024], F32, kind="ExternalInput").ap()
    muk_d = nc.dram_tensor("muk", [1024], F32, kind="ExternalInput").ap()
    dbg_d = None
    if DEBUG_DUMP:
        dbg_d = nc.dram_tensor("dbg", [128, 4096], BF16, kind="ExternalOutput").ap()
        dbgo_d = nc.dram_tensor("dbgo", [128, 4096], F32, kind="ExternalOutput").ap()
        dbgr_d = nc.dram_tensor("dbgr", [128, 512], F32, kind="ExternalOutput").ap()
        dbgy_d = nc.dram_tensor("dbgy", [128, 2048], BF16, kind="ExternalOutput").ap()

    prog = Prog()

    def natural_ap(dram, rows, cols):
        return bass.AP(dram.tensor, 0, [[cols, 128], [128 * cols, rows // 128], [1, cols]])

    with ExitStack() as es:
        ec = es.enter_context
        ident = ec(nc.sbuf_tensor("ident", [128, 128], BF16))
        negmask = ec(nc.sbuf_tensor("negmask", [128, 128], BF16))
        ones_col = ec(nc.sbuf_tensor("ones_col", [128, 1], BF16))
        ones_row = ec(nc.sbuf_tensor("ones_row", [1, 128], F32))
        mu_r4 = ec(nc.sbuf_tensor("mu_r4", [128, 8], F32))
        mu_k4 = ec(nc.sbuf_tensor("mu_k4", [128, 8], F32))
        mu1m_r4 = ec(nc.sbuf_tensor("mu1m_r4", [128, 8], F32))
        mu1m_k4 = ec(nc.sbuf_tensor("mu1m_k4", [128, 8], F32))
        wdkvT = ec(nc.sbuf_tensor("wdkvT", [128, 8 * 512], BF16))
        m1 = ec(nc.sbuf_tensor("m1", [128, 8 * 1024], BF16))
        keff = ec(nc.sbuf_tensor("keff", [128, 8 * 512], BF16))
        vo = ec(nc.sbuf_tensor("vo", [128, 4 * 1024], BF16))
        wrT = ec(nc.sbuf_tensor("wrT", [128, 8 * 1024], BF16))
        wkT = ec(nc.sbuf_tensor("wkT", [128, 8 * 1024], BF16))
        wvT = ec(nc.sbuf_tensor("wvT", [128, 8 * 1024], BF16))

        bufs = {}
        def B(name):
            if name not in bufs:
                bufs[name] = Buf()
            return bufs[name]

        with ExitStack() as ps:
            pc = ps.enter_context
            stg_f = pc(nc.sbuf_tensor("stg_f", [128, 8192], F32))
            stg_b = pc(nc.sbuf_tensor("stg_b", [128, 8192], BF16))
            wuqT = pc(nc.sbuf_tensor("wuqT", [128, 4 * 1024], BF16))
            wdq_st = pc(nc.sbuf_tensor("wdq_st", [128, 4 * 1024], BF16))
            m1T = pc(nc.sbuf_tensor("m1T", [128, 8 * 1024], BF16))
            wuk_st = pc(nc.sbuf_tensor("wuk_st", [128, 8 * 512], BF16))
            wuv_st = pc(nc.sbuf_tensor("wuv_st", [128, 8 * 512], BF16))
            woT = pc(nc.sbuf_tensor("woT", [128, 8 * 1024], BF16))
            pp0 = pc(nc.psum_tensor("pp0", [128, 512], F32))
            pp1 = pc(nc.psum_tensor("pp1", [128, 512], F32))
            ppt = pc(nc.psum_tensor("ppt", [128, 512], BF16))
            ppt2 = pc(nc.psum_tensor("ppt2", [128, 512], BF16))
            pps = [pp0, pp1]
            ppb = [B("pp0"), B("pp1")]

            def P(eng, fn, reads=(), writes=(), dma=False, accum=False):
                return prog.op(eng, fn, reads, writes, dma=dma, accum=accum, phase=0)

            P("gpsimd", lambda q: q.iota(stg_f[:, 0:128], [[1, 128]],
                                         channel_multiplier=-1,
                                         allow_small_or_imprecise_dtypes=True),
              writes=[B("stg_f0")])
            P("vector", lambda q: q.tensor_single_scalar(
                stg_f[:, 128:256], stg_f[:, 0:128], 0.0, ALU.is_equal),
              reads=[B("stg_f0")], writes=[B("stg_f0")])
            P("scalar", lambda q: q.activation(ident[:], stg_f[:, 128:256], AF.Copy),
              reads=[B("stg_f0")], writes=[B("ident")])
            P("vector", lambda q: q.tensor_single_scalar(
                negmask[:], stg_f[:, 0:128], 0.0, ALU.is_ge),
              reads=[B("stg_f0")], writes=[B("negmask")])
            P("vector", lambda q: q.memset(ones_col[:], 1.0), writes=[B("ones_col")])
            P("vector", lambda q: q.memset(ones_row[:], 1.0), writes=[B("ones_row")])
            mur_ap = bass.AP(mur_d.tensor, 0, [[1, 128], [128, 8]])
            muk_ap = bass.AP(muk_d.tensor, 0, [[1, 128], [128, 8]])
            P("sync", lambda q: q.dma_start(mu_r4[:], mur_ap,
                                            allow_slow_non_contiguous=True),
              writes=[B("mu_r4")], dma=True)
            P("sync", lambda q: q.dma_start(mu_k4[:], muk_ap,
                                            allow_slow_non_contiguous=True),
              writes=[B("mu_k4")], dma=True)
            P("vector", lambda q: q.tensor_scalar(mu1m_r4[:], mu_r4[:], -1.0, 1.0,
                                                  ALU.mult, ALU.add),
              reads=[B("mu_r4")], writes=[B("mu1m_r4")])
            P("vector", lambda q: q.tensor_scalar(mu1m_k4[:], mu_k4[:], -1.0, 1.0,
                                                  ALU.mult, ALU.add),
              reads=[B("mu_k4")], writes=[B("mu1m_k4")])

            stghalf = [0]
            stgfb = [B("stg_f0"), B("stg_f1")]
            stgbb = [B("stg_b0"), B("stg_b1")]
            ppts = [ppt, ppt2]
            pptb = [B("ppt0"), B("ppt1")]

            def load_half(dram, row0, nchunk, cols):
                h2 = stghalf[0] % 2
                stghalf[0] += 1
                src = bass.AP(dram.tensor, row0 * cols,
                              [[cols, 128], [128 * cols, nchunk], [1, cols]])
                dst = bass.AP(stg_f, h2 * 4096, [[8192, 128], [cols, nchunk], [1, cols]])
                P("sync" if h2 == 0 else "scalar",
                  lambda q, s=src, d=dst: q.dma_start(d, s),
                  writes=[stgfb[h2]], dma=True)
                return h2, stg_f[:, h2 * 4096: h2 * 4096 + nchunk * cols]

            def load_convert(dram, rows, dst_ap, dst_buf):
                cols = dram.shape[1]
                nchunk = rows // 128
                assert nchunk * cols <= 4096
                h2, view_f = load_half(dram, 0, nchunk, cols)
                half_elems = nchunk * cols // 2
                tok = B(f"cvt_tok_{dram.name}")
                P("scalar", lambda q, a=dst_ap[:, 0:half_elems],
                  b=view_f[:, 0:half_elems]: q.activation(a, b, AF.Copy),
                  reads=[stgfb[h2]], writes=[dst_buf, tok])
                P("vector", lambda q, a=dst_ap[:, half_elems:2 * half_elems],
                  b=view_f[:, half_elems:2 * half_elems]: q.tensor_copy(a, b),
                  reads=[stgfb[h2], tok], writes=[dst_buf])

            def load_transpose(dram, rows, chW, dstT, dstT_buf, dst_cw):
                cols = chW
                nr = rows // 128
                nch = cols // 128
                nhalves = max((nr * cols + 4095) // 4096, (nr + 3) // 4)
                rows_per_half = nr // nhalves
                for hh in range(nhalves):
                    h2, view_f = load_half(dram, hh * rows_per_half * 128,
                                           rows_per_half, cols)
                    nn = rows_per_half * cols
                    vb1 = stg_b[:, h2 * 4096: h2 * 4096 + nn // 2]
                    vb2 = stg_b[:, h2 * 4096 + nn // 2: h2 * 4096 + nn]
                    tok = B(f"cvt_tok_b{h2}")
                    P("scalar", lambda q, a=vb1, b=view_f[:, 0:nn // 2]:
                      q.activation(a, b, AF.Copy),
                      reads=[stgfb[h2]], writes=[stgbb[h2], tok])
                    P("vector", lambda q, a=vb2, b=view_f[:, nn // 2:nn]:
                      q.tensor_copy(a, b),
                      reads=[stgfb[h2], tok], writes=[stgbb[h2]])
                    for ci in range(nch):
                        pt = ci % 2
                        for rj in range(rows_per_half):
                            src_t = stg_b[:, h2 * 4096 + rj * cols + ci * 128:
                                          h2 * 4096 + rj * cols + ci * 128 + 128]
                            dst_t = ppts[pt][:, rj * 128: rj * 128 + 128]
                            P("tensor",
                              lambda q, a=dst_t, b=src_t: q.transpose(a, b, ident[:]),
                              reads=[stgbb[h2], B("ident")], writes=[pptb[pt]],
                              accum=(rj > 0))
                        dst_sl = dstT[:, ci * dst_cw + hh * rows_per_half * 128:
                                      ci * dst_cw + (hh + 1) * rows_per_half * 128]
                        src_sl = ppts[pt][:, 0:rows_per_half * 128]
                        P("vector", lambda q, a=dst_sl, b=src_sl: q.tensor_copy(a, b),
                          reads=[pptb[pt]], writes=[dstT_buf])

            pi = 0
            def prep_mm(dst_ap, dst_buf, lhs_fn, rhs_fn, nk, reads, scale=None):
                nonlocal pi
                ps_t, ps_b = pps[pi % 2], ppb[pi % 2]
                pi += 1
                for k in range(nk):
                    P("tensor", lambda q, o=ps_t[:], l=lhs_fn(k), r=rhs_fn(k),
                      s1=(k == 0), s2=(k == nk - 1): q.matmul(o, l, r, start=s1, stop=s2),
                      reads=reads, writes=[ps_b], accum=(k > 0))
                if scale is None:
                    P("scalar", lambda q, a=dst_ap, b=ps_t[:]: q.activation(a, b, AF.Copy),
                      reads=[ps_b], writes=[dst_buf])
                else:
                    P("scalar", lambda q, a=dst_ap, b=ps_t[:], sc=scale:
                      q.activation(a, b, AF.Copy, scale=sc),
                      reads=[ps_b], writes=[dst_buf])

            load_transpose(wuq_d, 1024, 512, wuqT, B("wuqT"), 1024)
            load_convert(wdq_d, 512, wdq_st[:, 0:4 * 1024], B("wdq_st"))
            for ci in range(8):
                for h2 in range(2):
                    prep_mm(m1[:, ci * 1024 + h2 * 512: ci * 1024 + h2 * 512 + 512],
                            B("m1"),
                            lambda k, ci=ci: wdq_st[:, k * 1024 + ci * 128: k * 1024 + ci * 128 + 128],
                            lambda k, h2=h2: wuqT[:, k * 1024 + h2 * 512: k * 1024 + h2 * 512 + 512],
                            4, [B("wdq_st"), B("wuqT")])
            for ci in range(8):
                for h2 in range(2):
                    prep_mm(m1T[:, ci * 1024 + h2 * 512: ci * 1024 + h2 * 512 + 512],
                            B("m1T"),
                            lambda k, ci=ci: wuqT[:, k * 1024 + ci * 128: k * 1024 + ci * 128 + 128],
                            lambda k, h2=h2: wdq_st[:, k * 1024 + h2 * 512: k * 1024 + h2 * 512 + 512],
                            4, [B("wdq_st"), B("wuqT")])
            load_convert(wuk_d, 1024, wuk_st[:, 0:8 * 512], B("wuk_st"))
            for ri in range(8):
                prep_mm(keff[:, ri * 512: ri * 512 + 512], B("keff"),
                        lambda k, ri=ri: m1T[:, k * 1024 + ri * 128: k * 1024 + ri * 128 + 128],
                        lambda k: wuk_st[:, k * 512: k * 512 + 512],
                        8, [B("m1T"), B("wuk_st")], scale=0.125)
            load_convert(wuv_d, 1024, wuv_st[:, 0:8 * 512], B("wuv_st"))
            load_transpose(wdkv_d, 512, 1024, wdkvT, B("wdkvT"), 512)
            load_transpose(wo_d, 1024, 1024, woT, B("woT"), 1024)
            load_transpose(wr_d, 1024, 1024, wrT, B("wrT"), 1024)
            load_transpose(wk_d, 1024, 1024, wkT, B("wkT"), 1024)
            load_transpose(wv_d, 1024, 1024, wvT, B("wvT"), 1024)
            for li in range(4):
                for h2 in range(2):
                    prep_mm(vo[:, li * 1024 + h2 * 512: li * 1024 + h2 * 512 + 512],
                            B("vo"),
                            lambda k, li=li: wuv_st[:, k * 512 + li * 128: k * 512 + li * 128 + 128],
                            lambda k, h2=h2: woT[:, k * 1024 + h2 * 512: k * 1024 + h2 * 512 + 512],
                            8, [B("wuv_st"), B("woT")])

        ckvT = ec(nc.sbuf_tensor("ckvT", [128, 4 * 2048], BF16))
        ckvtok = ec(nc.sbuf_tensor("ckvtok", [128, 16 * 512], BF16))
        xnT = ec(nc.sbuf_tensor("xnT", [128, 8 * 512], BF16))
        qT = ec(nc.sbuf_tensor("qT", [128, 8 * 512], BF16))
        pcol0 = ec(nc.sbuf_tensor("pcol0", [128, 8], BF16))
        pcol1 = ec(nc.sbuf_tensor("pcol1", [128, 8], BF16))
        outacc = ec(nc.sbuf_tensor("outacc", [128, 4096], F32))
        qlat = ec(nc.sbuf_tensor("qlat", [128, 4 * 512], BF16))
        qlat1 = ec(nc.sbuf_tensor("qlat1", [128, 4 * 512], BF16))
        a_t0 = ec(nc.sbuf_tensor("a_t0", [128, 512], BF16))
        a_t1 = ec(nc.sbuf_tensor("a_t1", [128, 512], BF16))
        a_t2 = ec(nc.sbuf_tensor("a_t2", [128, 512], BF16))
        a_t3 = ec(nc.sbuf_tensor("a_t3", [128, 512], BF16))
        a_t4 = ec(nc.sbuf_tensor("a_t4", [128, 512], BF16))
        a_t5 = ec(nc.sbuf_tensor("a_t5", [128, 512], BF16))
        ylat = ec(nc.sbuf_tensor("ylat", [128, 4 * 512], BF16))
        recip = ec(nc.sbuf_tensor("recip", [128, 512], F32))
        scr512 = ec(nc.sbuf_tensor("scr512", [128, 512], F32))
        xr_g = ec(nc.sbuf_tensor("xr_g", [128, 8 * 512], BF16))
        xk_g = ec(nc.sbuf_tensor("xk_g", [128, 8 * 512], BF16))
        r_tile = recip
        bufs["r_tile"] = B("recip")
        kact = qT
        bufs["kact"] = B("qT")
        xt0 = ec(nc.sbuf_tensor("xt0", [128, 1024], F32))
        xt1 = ec(nc.sbuf_tensor("xt1", [128, 1024], F32))
        xn0 = ec(nc.sbuf_tensor("xn0", [128, 1024], BF16))
        xn1 = ec(nc.sbuf_tensor("xn1", [128, 1024], BF16))
        xn2 = ec(nc.sbuf_tensor("xn2", [128, 1024], BF16))
        xn3 = ec(nc.sbuf_tensor("xn3", [128, 1024], BF16))
        rc = ec(nc.sbuf_tensor("rc", [128, 8], F32))
        srow = recip
        bufs["srow"] = B("recip")
        stats = [[ec(nc.sbuf_tensor(f"st{n}{j}", [128, w], F32))
                  for n, w in (("bn", 12), ("ag", 2), ("rstd", 1), ("nbias", 1))]
                 for j in range(2)]
        statb = [[B(f"st{n}{j}") for n in ("bn", "ag", "rstd", "nbias")]
                 for j in range(2)]

        ps_a = ec(nc.psum_tensor("ps_a", [128, 512], F32))
        ps_b = ec(nc.psum_tensor("ps_b", [128, 512], F32))
        ps_y = [ec(nc.psum_tensor(f"ps_y{i}", [128, 512], F32)) for i in range(4)]
        ps_m = ec(nc.psum_tensor("ps_m", [128, 512], F32))
        ps_bc = ec(nc.psum_tensor("ps_bc", [128, 512], F32))

        pst = [ps_a, ps_b]
        psb = [B("ps_a"), B("ps_b")]
        pyb = [B(f"ps_y{i}") for i in range(4)]
        pst4 = [ps_a, ps_b, ps_y[0], ps_y[1]]
        psb4 = [B("ps_a"), B("ps_b"), pyb[0], pyb[1]]
        pmb = B("ps_m")
        pbcb = B("ps_bc")
        xts = [xt0, xt1]
        xtb = [B("xt0"), B("xt1")]
        xns = [xn0, xn1, xn2, xn3]
        xnb = [B("xn0"), B("xn1"), B("xn2"), B("xn3")]
        ats = [a_t0, a_t1, a_t2, a_t3, a_t4, a_t5]
        atb = [B("a_t0"), B("a_t1"), B("a_t2"), B("a_t3"), B("a_t4"), B("a_t5")]
        NAT = 6
        qlats = [qlat, qlat1]
        qlb = [[B(f"qlat{j}_{i}") for i in range(4)] for j in range(2)]
        ylb = [B(f"ylat{i}") for i in range(4)]
        ylb0s = [B(f"ylat0s{i}") for i in range(4)]
        pcols = [pcol0, pcol1]
        pcolb = [B("pcol0"), B("pcol1")]

        def M(eng, fn, reads=(), writes=(), dma=False, accum=False):
            return prog.op(eng, fn, reads, writes, dma=dma, accum=accum, phase=1)

        store_ids = []
        mm_ping = [0]

        def emit_mix_mult(g):
            xn3 = xnT[:].rearrange("p (c t) -> p c t", c=8)
            for dst, dstb, mu_t, mub in (
                    (xr_g, B("xr_g"), mu_r4, B("mu_r4")),
                    (xk_g, B("xk_g"), mu_k4, B("mu_k4"))):
                mu_bc = bass.AP(mu_t, 0, [[8, 128], [1, 8], [0, 512]])
                d3 = dst[:].rearrange("p (c t) -> p c t", c=8)
                M("gpsimd", lambda q, a=d3, b=xn3, c2=mu_bc:
                  q.tensor_tensor(a, b, c2, ALU.mult),
                  reads=[B("xnT"), mub], writes=[dstb])

        def emit_mix_chunk(g, ci):
            pc, pcb = pcols[g % 2], pcolb[g % 2]
            for dst, dstb, mu1m_t, mu1mb in (
                    (xr_g, B("xr_g"), mu1m_r4, B("mu1m_r4")),
                    (xk_g, B("xk_g"), mu1m_k4, B("mu1m_k4"))):
                o_ap = dst[:, ci * 512 + 1: ci * 512 + 512]
                i_ap = xnT[:, ci * 512: ci * 512 + 511]
                sc = mu1m_t[:, ci: ci + 1]
                M("vector", lambda q, a=o_ap, b=i_ap, s1=sc:
                  q.scalar_tensor_tensor(a, b, s1, a, ALU.mult, ALU.add),
                  reads=[B("xnT"), mu1mb, dstb], writes=[dstb])
                o0 = dst[:, ci * 512: ci * 512 + 1]
                p0_ap = pc[:, ci: ci + 1]
                M("vector", lambda q, a=o0, b=p0_ap, s1=sc:
                  q.scalar_tensor_tensor(a, b, s1, a, ALU.mult, ALU.add),
                  reads=[pcb, mu1mb, dstb], writes=[dstb])

        def pe_acc(dst_ap_fn, dst_buf, lhs_list, rhs_list, reads, nbank=2):
            ps_t, ps_bb = pst4[mm_ping[0] % nbank], psb4[mm_ping[0] % nbank]
            mm_ping[0] += 1
            n = len(lhs_list)
            for k in range(n):
                M("tensor", lambda q, o=ps_t[:], l=lhs_list[k], r=rhs_list[k],
                  s1=(k == 0), s2=(k == n - 1): q.matmul(o, l, r, start=s1, stop=s2),
                  reads=reads, writes=[ps_bb], accum=(k > 0))
            return ps_t, ps_bb


        iters = [(s, g) for s in range(B_LOC) for g in range(NGRP)]

        def emit_ln_tile(s, g, i):
            t0 = (g * NTIL + i) * 128
            xt, xb = xts[i % 2], xtb[i % 2]
            xn, xnbb = xns[i], xnb[i]
            (bn, ag, rstd, nbias) = stats[i % 2]
            (bn_b, ag_b, rstd_b, nbias_b) = statb[i % 2]
            M("sync", lambda q, d=xt[:], sl=x_d[s, t0:t0 + 128, :]:
              q.dma_start(d, sl), writes=[xb], dma=True)
            M("vector", lambda q, a=bn[:, 0:6], b=xt[:, 0:512]: q.bn_stats(a, b),
              reads=[xb], writes=[bn_b])
            M("vector", lambda q, a=bn[:, 6:12], b=xt[:, 512:1024]: q.bn_stats(a, b),
              reads=[xb], writes=[bn_b])
            M("vector", lambda q, a=ag[:], b=bn[:]: q.bn_aggr(a, b),
              reads=[bn_b], writes=[ag_b])
            M("vector", lambda q, a=ag[:, 1:2]: q.tensor_scalar_add(a, a, 1.0e-5),
              reads=[ag_b], writes=[ag_b])
            M("scalar", lambda q, a=ag[:, 1:2]: q.activation(a, a, AF.Ln),
              reads=[ag_b], writes=[ag_b])
            M("scalar", lambda q, a=rstd[:], b=ag[:, 1:2]:
              q.activation(a, b, AF.Exp, scale=-0.5),
              reads=[ag_b], writes=[rstd_b])
            M("vector", lambda q, a=nbias[:], b=ag[:, 0:1], c2=rstd[:]:
              q.scalar_tensor_tensor(a, b, -1.0, c2, ALU.mult, ALU.mult),
              reads=[ag_b, rstd_b], writes=[nbias_b])
            nbias_bc = bass.AP(nbias, 0, [[1, 128], [0, 1024]])
            M("vector", lambda q, a=xn[:], b=xt[:], nb=nbias_bc, sc=rstd[:]:
              q.scalar_tensor_tensor(a, b, sc, nb, ALU.mult, ALU.add),
              reads=[xb, nbias_b, rstd_b], writes=[xnbb])

        def emit_A(s, g):
            ckvT_b = B(f"ckvT_g{g}")
            ckvtok_b = B(f"ckvtok_g{g}")
            for i in range(NTIL):
                xn, xnbb = xns[i], xnb[i]
                ps_t, ps_bb = pst[i % 2], psb[i % 2]
                ps_v = ps_t[:].bitcast(BF16)
                for ci in range(8):
                    M("tensor", lambda q, a=ps_v[:, ci * 128:(ci + 1) * 128],
                      b=xn[:, ci * 128:(ci + 1) * 128]:
                      q.transpose(a, b, ident[:]),
                      reads=[xnbb, B("ident")], writes=[ps_bb],
                      accum=(ci > 0))
                src = ps_v.rearrange("p (c t) -> p c t", c=8)
                dst = bass.AP(xnT, i * 128, [[4096, 128], [512, 8], [1, 128]])
                M("scalar", lambda q, a=dst, b=src: q.activation(a, b, AF.Copy),
                  reads=[ps_bb], writes=[B("xnT")])
            if g < NGRP - 1:
                src = bass.AP(xnT, 511, [[4096, 128], [512, 8], [1, 1]])
                dst = bass.AP(pcols[(g + 1) % 2], 0, [[8, 128], [1, 8], [1, 1]])
                M("vector", lambda q, a=dst, b=src: q.tensor_copy(a, b),
                  reads=[B("xnT")], writes=[pcolb[(g + 1) % 2]])
            for lc in range(4):
                ps_t, ps_bb = pe_acc(None, None,
                    [wdkvT[:, ci * 512 + lc * 128: ci * 512 + lc * 128 + 128] for ci in range(8)],
                    [xnT[:, ci * 512: ci * 512 + 512] for ci in range(8)],
                    [B("wdkvT"), B("xnT")], nbank=4)
                dst = ckvT[:, lc * 2048 + g * 512: lc * 2048 + g * 512 + 512]
                M("scalar", lambda q, a=dst, b=ps_t[:]: q.activation(a, b, AF.Copy),
                  reads=[ps_bb], writes=[ckvT_b])
            for tt in range(NTIL):
                ps_t, ps_bb = pe_acc(None, None,
                    [xnT[:, ci * 512 + tt * 128: ci * 512 + tt * 128 + 128] for ci in range(8)],
                    [wdkvT[:, ci * 512: ci * 512 + 512] for ci in range(8)],
                    [B("wdkvT"), B("xnT")], nbank=4)
                dst = ckvtok[:, (g * NTIL + tt) * 512: (g * NTIL + tt) * 512 + 512]
                M("scalar", lambda q, a=dst, b=ps_t[:]: q.activation(a, b, AF.Copy),
                  reads=[ps_bb], writes=[ckvtok_b])
            for cqc in range(8):
                ps_t, ps_bb = pe_acc(None, None,
                    [m1[:, ci * 1024 + cqc * 128: ci * 1024 + cqc * 128 + 128] for ci in range(8)],
                    [xnT[:, ci * 512: ci * 512 + 512] for ci in range(8)],
                    [B("m1"), B("xnT")], nbank=4)
                dst = qT[:, cqc * 512: cqc * 512 + 512]
                M("vector", lambda q, a=dst, b=ps_t[:]: q.tensor_copy(a, b),
                  reads=[ps_bb], writes=[B("qT")])

        srow_ap = recip[0:1, 0:512]

        def emit_bc_rows(h):
            M("scalar", lambda q: q.activation(srow_ap, ps_m[0:1, 0:512], AF.Ln),
              reads=[pmb], writes=[B("recip")])
            M("scalar", lambda q: q.activation(srow_ap, srow_ap, AF.Exp, scale=-1.0),
              reads=[B("recip")], writes=[B("recip")])

        def emit_bc_rest(h):
            for sub in range(4):
                M("tensor", lambda q, o=ps_bc[:, sub:sub + 1],
                  l=srow_ap[:, sub * 128:(sub + 1) * 128],
                  s1=(sub == 0), s2=(sub == 3):
                  q.matmul(o, l, ones_row[0:1, 0:1], start=s1, stop=s2),
                  reads=[B("ones_row"), B("recip")], writes=[pbcb],
                  accum=(sub > 0))
            rcd = rc[:, (h % 2) * 4:(h % 2) * 4 + 4]
            M("vector", lambda q, a=rcd, b=ps_bc[:, 0:4]: q.tensor_copy(a, b),
              reads=[pbcb], writes=[B("rc")])
            for c4 in range(4):
                M("vector", lambda q, a=ylat[:, c4 * 128:(c4 + 1) * 128],
                  b=ps_y[0][:, c4 * 128:(c4 + 1) * 128]:
                  q.tensor_copy(a, b),
                  reads=[pyb[0]], writes=[ylb0s[c4]])
            for lc in range(1, 4):
                M("vector", lambda q, a=ylat[:, lc * 512:(lc + 1) * 512],
                  b=ps_y[lc][:]: q.tensor_copy(a, b),
                  reads=[pyb[lc]], writes=[ylb[lc]])

        def emit_y_lc(h, lc):
            for sub in range(4):
                rb = ylb0s[sub] if lc == 0 else ylb[lc]
                M("tensor", lambda q,
                  o=ps_m[:, sub * 64: sub * 64 + 64],
                  l=ylat[:, lc * 512 + sub * 128: lc * 512 + sub * 128 + 128],
                  r=vo[:, lc * 1024 + h * 64: lc * 1024 + h * 64 + 64],
                  s1=(lc == 0 and sub == 0), s2=(lc == 3 and sub == 3):
                  q.matmul(o, l, r, start=s1, stop=s2),
                  reads=[rb, B("vo")], writes=[pmb], accum=(lc > 0 or sub > 0))

        def emit_y_evac(h):
            ysrc = bass.AP(ps_m, 0, [[512, 128], [64, 4], [1, 64]])
            ydst = bass.AP(outacc, h * 64, [[4096, 128], [1024, 4], [1, 64]])
            rc_bc = bass.AP(rc, (h % 2) * 4, [[8, 128], [1, 4], [0, 64]])
            M("vector", lambda q, a=ydst, b=ysrc, c2=rc_bc:
              q.tensor_tensor(a, b, c2, ALU.mult),
              reads=[pmb, B("rc")], writes=[B("outacc")])

        def emit_y(h):
            for lc in range(4):
                emit_y_lc(h, lc)
            emit_y_evac(h)

        def emit_k_ffn(g):
            for dkc in range(8):
                ps_t, ps_bb = pe_acc(None, None,
                    [wkT[:, ci * 1024 + dkc * 128: ci * 1024 + dkc * 128 + 128] for ci in range(8)],
                    [xk_g[:, ci * 512: ci * 512 + 512] for ci in range(8)],
                    [B("wkT"), B("xk_g")], nbank=4)
                M("scalar", lambda q, a=scr512[:], b=ps_t[:]: q.activation(a, b, AF.Relu),
                  reads=[ps_bb], writes=[B("scr512")])
                M("scalar", lambda q, a=kact[:, dkc * 512:(dkc + 1) * 512], b=scr512[:]:
                  q.activation(a, b, AF.Square),
                  reads=[B("scr512")], writes=[B("kact")])

        def emit_B(s, g, early_ln):
            K = 4 * g + 4
            LEAD = 2
            pend = [None]

            def scores(h, kt):
                q0 = max(0, (kt - 4 * g) * 128)
                ps_t, ps_bb = pst[kt % 2], psb[kt % 2]
                ckv_rb = B(f"ckvT_g{kt // 4}")
                ql_t = qlats[h % 2]
                for lc in range(4):
                    lhs = ckvT[:, lc * 2048 + kt * 128: lc * 2048 + kt * 128 + 128]
                    rhs = ql_t[:, lc * 512 + q0: lc * 512 + 512]
                    M("tensor", lambda q, o=ps_t[:, q0:512], l=lhs, r=rhs,
                      s1=(lc == 0), s2=(lc == 3):
                      q.matmul(o, l, r, start=s1, stop=s2),
                      reads=[ckv_rb, qlb[h % 2][lc]], writes=[ps_bb], accum=(lc > 0))

            def qlat_stage(hh, lc, bank=None, evac="vector"):
                p0 = (hh % 2) * 64
                hc = hh // 2
                ps_t, ps_bb = (ps_bc, pbcb) if bank is None else bank
                lhs = keff[p0:p0 + 64, hc * 512 + lc * 128: hc * 512 + lc * 128 + 128]
                rhs = qT[p0:p0 + 64, hc * 512: hc * 512 + 512]
                M("tensor", lambda q, o=ps_t[:], l=lhs, r=rhs:
                  q.matmul(o, l, r, start=True, stop=True),
                  reads=[B("keff"), B("qT")], writes=[ps_bb])
                ql_t = qlats[hh % 2]
                if evac == "vector":
                    M("vector", lambda q, a=ql_t[:, lc * 512:(lc + 1) * 512],
                      b=ps_t[:]: q.tensor_copy(a, b),
                      reads=[ps_bb], writes=[qlb[hh % 2][lc]])
                else:
                    M("scalar", lambda q, a=ql_t[:, lc * 512:(lc + 1) * 512],
                      b=ps_t[:]: q.activation(a, b, AF.Copy),
                      reads=[ps_bb], writes=[qlb[hh % 2][lc]])

            def maskexp(kt):
                q0 = max(0, (kt - 4 * g) * 128)
                ps_t, ps_bb = pst[kt % 2], psb[kt % 2]
                at, ab = ats[kt % NAT], atb[kt % NAT]
                M("scalar", lambda q, a=at[:, q0:512], b=ps_t[:, q0:512]:
                  q.activation(a, b, AF.Exp),
                  reads=[ps_bb], writes=[ab])
                if kt >= 4 * g:
                    M("vector", lambda q, a=at[:, q0:q0 + 128]:
                      q.tensor_tensor(a, a, negmask[:], ALU.mult),
                      reads=[ab, B("negmask")], writes=[ab])

            def ylat_only(kt):
                q0 = max(0, (kt - 4 * g) * 128)
                at, ab = ats[kt % NAT], atb[kt % NAT]
                ctok_rb = B(f"ckvtok_g{kt // 4}")
                for lc in range(4):
                    lhs = ckvtok[:, kt * 512 + lc * 128: kt * 512 + lc * 128 + 128]
                    M("tensor", lambda q, o=ps_y[lc][:, q0:512], l=lhs,
                      r=at[:, q0:512], s1=(kt == 0), s2=(kt == K - 1):
                      q.matmul(o, l, r, start=s1, stop=s2),
                      reads=[ctok_rb, ab], writes=[pyb[lc]], accum=(kt > 0))

            def sums_only(kt):
                q0 = max(0, (kt - 4 * g) * 128)
                at, ab = ats[kt % NAT], atb[kt % NAT]
                M("tensor", lambda q, o=ps_m[0:1, q0:512], l=ones_col[:],
                  r=at[:, q0:512], s1=(kt == 0), s2=(kt == K - 1):
                  q.matmul(o, l, r, start=s1, stop=s2),
                  reads=[B("ones_col"), ab], writes=[pmb], accum=(kt > 0))

            def ylat_mm(kt):
                ylat_only(kt)
                sums_only(kt)

            for h in range(NH):
                if h == 0:
                    for lc in range(4):
                        qlat_stage(0, lc)
                for hh, fn in early_ln:
                    if hh == h:
                        fn()
                scores(h, 0)
                maskexp(0)
                scores(h, 1)
                maskexp(1)
                if pend[0] is not None:
                    emit_bc_rest(pend[0])
                scores(h, 2)
                maskexp(2)
                scores(h, 3)
                maskexp(3)
                if K == 4 and h + 1 < NH:
                    g0banks = [None, (ps_m, pmb), (ps_a, psb[0]), (ps_b, psb[1])]
                    for lc in range(4):
                        qlat_stage(h + 1, lc, bank=g0banks[lc],
                                   evac="scalar" if lc % 2 == 0 else "vector")
                if pend[0] is not None:
                    emit_y(pend[0])
                    pend[0] = None
                nyl = [0]

                def after_ylat():
                    if K > 4 and h + 1 < NH and nyl[0] < 4:
                        qlat_stage(h + 1, nyl[0])
                    nyl[0] += 1
                LEAD = 5 if K > 4 else 4
                if K > 4:
                    scores(h, 4)
                    maskexp(4)
                for kt in range(LEAD, K):
                    scores(h, kt)
                    maskexp(kt)
                    ylat_mm(kt - LEAD)
                    after_ylat()
                for kt in range(max(0, K - LEAD), K):
                    sums_only(kt)
                emit_bc_rows(h)
                for kt in range(max(0, K - LEAD), K):
                    ylat_only(kt)
                    after_ylat()
                pend[0] = h
            emit_bc_rest(pend[0])
            emit_k_ffn(g)
            emit_y(pend[0])
            pend[0] = None

        def emit_C(s, g):
            for tt in range(NTIL):
                t0 = (g * NTIL + tt) * 128
                xt, xb = xts[tt % 2], xtb[tt % 2]
                M("sync", lambda q, d=xt[:], sl=x_d[s, t0:t0 + 128, :]:
                  q.dma_start(d, sl), writes=[xb], dma=True)
                for dh in range(2):
                    ps_t, ps_bb = pe_acc(None, None,
                        [xr_g[:, ci * 512 + tt * 128: ci * 512 + tt * 128 + 128] for ci in range(8)],
                        [wrT[:, ci * 1024 + dh * 512: ci * 1024 + dh * 512 + 512] for ci in range(8)],
                        [B("wrT"), B("xr_g")], nbank=4)
                    rh = r_tile[:, 0:512]
                    M("scalar", lambda q, a=rh, b=ps_t[:]:
                      q.activation(a, b, AF.Exp, scale=-1.0),
                      reads=[ps_bb], writes=[B("r_tile")])
                    M("vector", lambda q, a=rh: q.tensor_scalar_add(a, a, 1.0),
                      reads=[B("r_tile")], writes=[B("r_tile")])
                    M("scalar", lambda q, a=rh: q.activation(a, a, AF.Ln),
                      reads=[B("r_tile")], writes=[B("r_tile")])
                    M("scalar", lambda q, a=rh: q.activation(a, a, AF.Exp, scale=-1.0),
                      reads=[B("r_tile")], writes=[B("r_tile")])
                    ps_t, ps_bb = pe_acc(None, None,
                        [kact[:, dkc * 512 + tt * 128: dkc * 512 + tt * 128 + 128] for dkc in range(8)],
                        [wvT[:, dkc * 1024 + dh * 512: dkc * 1024 + dh * 512 + 512] for dkc in range(8)],
                        [B("wvT"), B("kact")], nbank=4)
                    M("vector", lambda q, a=scr512[:], b=rh, c2=ps_t[:]:
                      q.tensor_tensor(a, b, c2, ALU.mult),
                      reads=[B("r_tile"), ps_bb], writes=[B("scr512")])
                    oa = outacc[:, tt * 1024 + dh * 512: tt * 1024 + dh * 512 + 512]
                    M("vector", lambda q, a=oa: q.tensor_tensor(a, a, scr512[:], ALU.add),
                      reads=[B("scr512"), B("outacc")], writes=[B("outacc")])
                oa = outacc[:, tt * 1024: tt * 1024 + 1024]
                M("vector", lambda q, a=oa, b=xt[:]: q.tensor_tensor(a, a, b, ALU.add),
                  reads=[xb, B("outacc")], writes=[B("outacc")])
            g0t = g * TG
            dsts = bass.AP(out_d.tensor, (s * T + g0t) * 1024,
                           [[1024, 128], [128 * 1024, 4], [1, 1024]])
            srcs = outacc[:].rearrange("p (c t) -> p c t", c=4)
            sid = M("scalar", lambda q, d=dsts, b=srcs: q.dma_start(d, b),
                    reads=[B("outacc")], dma=True)
            store_ids.append(sid)

        for i in range(NTIL):
            emit_ln_tile(iters[0][0], iters[0][1], i)
        for it, (s, g) in enumerate(iters):
            if g == 0:
                M("vector", lambda q: q.memset(pcol0[:], 0.0), writes=[pcolb[0]])
            emit_A(s, g)
            if DEBUG_DUMP and it == 0:
                sid = M("scalar", lambda q: q.dma_start(dbg_d[:, :], xnT[:]),
                        reads=[B("xnT")], dma=True)
                store_ids.append(sid)
            early = [(1, lambda gg=g: emit_mix_mult(gg))]
            for ci in range(8):
                early.append((2 + ci, lambda gg=g, cc=ci: emit_mix_chunk(gg, cc)))
            if it + 1 < len(iters):
                s2, g2 = iters[it + 1]
                for i in range(NTIL):
                    early.append((6 + 2 * i,
                                  lambda ss=s2, gg=g2, ii=i: emit_ln_tile(ss, gg, ii)))
            emit_B(s, g, early)
            if DEBUG_DUMP and it == 0:
                sid = M("scalar", lambda q: q.dma_start(dbgo_d[:, :], outacc[:]),
                        reads=[B("outacc")], dma=True)
                store_ids.append(sid)
                sid = M("scalar", lambda q: q.dma_start(dbgr_d[:, :], recip[:]),
                        reads=[B("recip")], dma=True)
                store_ids.append(sid)
                sid = M("scalar", lambda q: q.dma_start(dbgy_d[:, :], ylat[:]),
                        reads=[ylb[0], ylb[1], ylb[2], ylb[3]], dma=True)
                store_ids.append(sid)
            emit_C(s, g)

        prog.finalize(extra_targets=store_ids)

        sems = {}
        for key in [("tensor", False), ("vector", False), ("scalar", False),
                    ("gpsimd", False), ("sync", True), ("scalar", True),
                    ("sync", False)]:
            sems[key] = ec(nc.semaphore(f"sem_{key[0]}_{'d' if key[1] else 'c'}"))

        with nc.Block() as b0:
            for eng in Prog.ENGINES:
                getattr(b0, eng)(prog.runner(eng, 0, sems))
        with nc.Block() as b1:
            for eng in Prog.ENGINES:
                getattr(b1, eng)(prog.runner(eng, 1, sems))
        sc_total = prog.sem_totals[("scalar", True)]
        with nc.Block() as b2:
            @b2.scalar
            def _(q):
                q.wait_ge(sems[("scalar", True)], sc_total)

    return nc


def kernel(**inputs):
    x = np.ascontiguousarray(inputs["L0_attn_x"], dtype=np.float32)
    nc = build_nc()
    weights = dict(
        wdq=np.ascontiguousarray(inputs["L0_attn_W_dq"], dtype=np.float32),
        wuq=np.ascontiguousarray(inputs["L0_attn_W_uq"], dtype=np.float32),
        wdkv=np.ascontiguousarray(inputs["L0_attn_W_dkv"], dtype=np.float32),
        wuk=np.ascontiguousarray(inputs["L0_attn_W_uk"], dtype=np.float32),
        wuv=np.ascontiguousarray(inputs["L0_attn_W_uv"], dtype=np.float32),
        wo=np.ascontiguousarray(inputs["L0_attn_W_o"], dtype=np.float32),
        wr=np.ascontiguousarray(inputs["L0_ffn_W_r"], dtype=np.float32),
        wk=np.ascontiguousarray(inputs["L0_ffn_W_k"], dtype=np.float32),
        wv=np.ascontiguousarray(inputs["L0_ffn_W_v"], dtype=np.float32),
        mur=np.ascontiguousarray(inputs["L0_ffn_mu_r"], dtype=np.float32),
        muk=np.ascontiguousarray(inputs["L0_ffn_mu_k"], dtype=np.float32),
    )
    in_maps = []
    for c in range(8):
        m = dict(weights)
        m["x"] = x[c * B_LOC:(c + 1) * B_LOC]
        in_maps.append(m)
    res = run_bass_kernel_spmd(nc, in_maps, core_ids=list(range(8)))
    return np.concatenate([r["out"] for r in res.results], axis=0)


# version 51
# speedup vs baseline: 1.1155x; 1.0218x over previous
import numpy as np
import concourse.bass as bass
import concourse.mybir as mybir
from concourse.bass_utils import run_bass_kernel_spmd
from contextlib import ExitStack

F32 = mybir.dt.float32
BF16 = mybir.dt.bfloat16
AF = mybir.ActivationFunctionType
ALU = mybir.AluOpType
AX = mybir.AxisListType

B_LOC = 4
DEBUG_DUMP = False
T = 2048
C = 1024
NH = 16
HS = 64
LAT = 512
TG = 512
NGRP = T // TG
NTIL = TG // 128


class Buf:
    __slots__ = ("last_write", "readers")

    def __init__(self):
        self.last_write = None
        self.readers = []


class Prog:
    ENGINES = ("tensor", "vector", "scalar", "sync", "gpsimd")

    def __init__(self):
        self.by_eng = {e: [] for e in self.ENGINES}
        self.nid = 0
        self.semval = {}
        self.maxw = {e: {} for e in self.ENGINES}

    def op(self, eng, fn, reads=(), writes=(), dma=False, accum=False, phase=0):
        deps = set()
        for b in reads:
            if b.last_write is not None:
                deps.add(b.last_write)
        if not accum:
            for b in writes:
                if b.last_write is not None:
                    deps.add(b.last_write)
                deps.update(b.readers)
        i = self.nid
        self.nid += 1
        self.by_eng[eng].append(
            dict(id=i, fn=fn, deps=deps, dma=dma, phase=phase))
        for b in writes:
            b.last_write = i
            if not accum:
                b.readers = []
        for b in reads:
            b.readers.append(i)
        return i

    def finalize(self, extra_targets=()):
        targets = set(extra_targets)
        for recs in self.by_eng.values():
            for r in recs:
                targets |= r["deps"]
        self.sem_totals = {}
        for eng, recs in self.by_eng.items():
            counts = {False: 0, True: 0}
            for r in recs:
                if r["id"] in targets:
                    counts[r["dma"]] += 16 if r["dma"] else 1
                    self.semval[r["id"]] = ((eng, r["dma"]), counts[r["dma"]])
            self.sem_totals[(eng, False)] = counts[False]
            self.sem_totals[(eng, True)] = counts[True]

    def runner(self, eng, phase, sems):
        recs = [r for r in self.by_eng[eng] if r["phase"] == phase]
        semval = self.semval
        maxw = self.maxw[eng]

        def run(q):
            for r in recs:
                for d in sorted(r["deps"]):
                    sv = semval.get(d)
                    if sv is None:
                        continue
                    k, v = sv
                    if maxw.get(k, 0) < v:
                        maxw[k] = v
                        q.wait_ge(sems[k], v)
                ins = r["fn"](q)
                sv = semval.get(r["id"])
                if sv is not None and ins is not None:
                    k, _ = sv
                    ins.then_inc(sems[k], 16 if r["dma"] else 1)
        return run


def build_nc():
    nc = bass.Bass("TRN2", target_bir_lowering=False)

    x_d = nc.dram_tensor("x", [B_LOC, T, C], F32, kind="ExternalInput").ap()
    out_d = nc.dram_tensor("out", [B_LOC, T, C], F32, kind="ExternalOutput").ap()
    wdq_d = nc.dram_tensor("wdq", [512, 1024], F32, kind="ExternalInput").ap()
    wuq_d = nc.dram_tensor("wuq", [1024, 512], F32, kind="ExternalInput").ap()
    wdkv_d = nc.dram_tensor("wdkv", [512, 1024], F32, kind="ExternalInput").ap()
    wuk_d = nc.dram_tensor("wuk", [1024, 512], F32, kind="ExternalInput").ap()
    wuv_d = nc.dram_tensor("wuv", [1024, 512], F32, kind="ExternalInput").ap()
    wo_d = nc.dram_tensor("wo", [1024, 1024], F32, kind="ExternalInput").ap()
    wr_d = nc.dram_tensor("wr", [1024, 1024], F32, kind="ExternalInput").ap()
    wk_d = nc.dram_tensor("wk", [1024, 1024], F32, kind="ExternalInput").ap()
    wv_d = nc.dram_tensor("wv", [1024, 1024], F32, kind="ExternalInput").ap()
    mur_d = nc.dram_tensor("mur", [1024], F32, kind="ExternalInput").ap()
    muk_d = nc.dram_tensor("muk", [1024], F32, kind="ExternalInput").ap()
    dbg_d = None
    if DEBUG_DUMP:
        dbg_d = nc.dram_tensor("dbg", [128, 4096], BF16, kind="ExternalOutput").ap()
        dbgo_d = nc.dram_tensor("dbgo", [128, 4096], F32, kind="ExternalOutput").ap()
        dbgr_d = nc.dram_tensor("dbgr", [128, 512], F32, kind="ExternalOutput").ap()
        dbgy_d = nc.dram_tensor("dbgy", [128, 2048], BF16, kind="ExternalOutput").ap()

    prog = Prog()

    def natural_ap(dram, rows, cols):
        return bass.AP(dram.tensor, 0, [[cols, 128], [128 * cols, rows // 128], [1, cols]])

    with ExitStack() as es:
        ec = es.enter_context
        ident = ec(nc.sbuf_tensor("ident", [128, 128], BF16))
        negmask = ec(nc.sbuf_tensor("negmask", [128, 128], BF16))
        ones_col = ec(nc.sbuf_tensor("ones_col", [128, 1], BF16))
        ones_row = ec(nc.sbuf_tensor("ones_row", [1, 128], F32))
        mu_r4 = ec(nc.sbuf_tensor("mu_r4", [128, 8], F32))
        mu_k4 = ec(nc.sbuf_tensor("mu_k4", [128, 8], F32))
        mu1m_r4 = ec(nc.sbuf_tensor("mu1m_r4", [128, 8], F32))
        mu1m_k4 = ec(nc.sbuf_tensor("mu1m_k4", [128, 8], F32))
        wdkvT = ec(nc.sbuf_tensor("wdkvT", [128, 8 * 512], BF16))
        m1 = ec(nc.sbuf_tensor("m1", [128, 8 * 1024], BF16))
        keff = ec(nc.sbuf_tensor("keff", [128, 8 * 512], BF16))
        vo = ec(nc.sbuf_tensor("vo", [128, 4 * 1024], BF16))
        wrT = ec(nc.sbuf_tensor("wrT", [128, 8 * 1024], BF16))
        wkT = ec(nc.sbuf_tensor("wkT", [128, 8 * 1024], BF16))
        wvT = ec(nc.sbuf_tensor("wvT", [128, 8 * 1024], BF16))

        bufs = {}
        def B(name):
            if name not in bufs:
                bufs[name] = Buf()
            return bufs[name]

        with ExitStack() as ps:
            pc = ps.enter_context
            stg_f = pc(nc.sbuf_tensor("stg_f", [128, 8192], F32))
            stg_b = pc(nc.sbuf_tensor("stg_b", [128, 8192], BF16))
            wuqT = pc(nc.sbuf_tensor("wuqT", [128, 4 * 1024], BF16))
            wdq_st = pc(nc.sbuf_tensor("wdq_st", [128, 4 * 1024], BF16))
            m1T = pc(nc.sbuf_tensor("m1T", [128, 8 * 1024], BF16))
            wuk_st = pc(nc.sbuf_tensor("wuk_st", [128, 8 * 512], BF16))
            wuv_st = pc(nc.sbuf_tensor("wuv_st", [128, 8 * 512], BF16))
            woT = pc(nc.sbuf_tensor("woT", [128, 8 * 1024], BF16))
            pp0 = pc(nc.psum_tensor("pp0", [128, 512], F32))
            pp1 = pc(nc.psum_tensor("pp1", [128, 512], F32))
            ppt = pc(nc.psum_tensor("ppt", [128, 512], BF16))
            ppt2 = pc(nc.psum_tensor("ppt2", [128, 512], BF16))
            pps = [pp0, pp1]
            ppb = [B("pp0"), B("pp1")]

            def P(eng, fn, reads=(), writes=(), dma=False, accum=False):
                return prog.op(eng, fn, reads, writes, dma=dma, accum=accum, phase=0)

            P("gpsimd", lambda q: q.iota(stg_f[:, 0:128], [[1, 128]],
                                         channel_multiplier=-1,
                                         allow_small_or_imprecise_dtypes=True),
              writes=[B("stg_f0")])
            P("vector", lambda q: q.tensor_single_scalar(
                stg_f[:, 128:256], stg_f[:, 0:128], 0.0, ALU.is_equal),
              reads=[B("stg_f0")], writes=[B("stg_f0")])
            P("scalar", lambda q: q.activation(ident[:], stg_f[:, 128:256], AF.Copy),
              reads=[B("stg_f0")], writes=[B("ident")])
            P("vector", lambda q: q.tensor_single_scalar(
                negmask[:], stg_f[:, 0:128], 0.0, ALU.is_ge),
              reads=[B("stg_f0")], writes=[B("negmask")])
            P("vector", lambda q: q.memset(ones_col[:], 1.0), writes=[B("ones_col")])
            P("vector", lambda q: q.memset(ones_row[:], 1.0), writes=[B("ones_row")])
            mur_ap = bass.AP(mur_d.tensor, 0, [[1, 128], [128, 8]])
            muk_ap = bass.AP(muk_d.tensor, 0, [[1, 128], [128, 8]])
            P("sync", lambda q: q.dma_start(mu_r4[:], mur_ap,
                                            allow_slow_non_contiguous=True),
              writes=[B("mu_r4")], dma=True)
            P("sync", lambda q: q.dma_start(mu_k4[:], muk_ap,
                                            allow_slow_non_contiguous=True),
              writes=[B("mu_k4")], dma=True)
            P("vector", lambda q: q.tensor_scalar(mu1m_r4[:], mu_r4[:], -1.0, 1.0,
                                                  ALU.mult, ALU.add),
              reads=[B("mu_r4")], writes=[B("mu1m_r4")])
            P("vector", lambda q: q.tensor_scalar(mu1m_k4[:], mu_k4[:], -1.0, 1.0,
                                                  ALU.mult, ALU.add),
              reads=[B("mu_k4")], writes=[B("mu1m_k4")])

            stghalf = [0]
            stgfb = [B("stg_f0"), B("stg_f1")]
            stgbb = [B("stg_b0"), B("stg_b1")]
            ppts = [ppt, ppt2]
            pptb = [B("ppt0"), B("ppt1")]

            def load_half(dram, row0, nchunk, cols):
                h2 = stghalf[0] % 2
                stghalf[0] += 1
                src = bass.AP(dram.tensor, row0 * cols,
                              [[cols, 128], [128 * cols, nchunk], [1, cols]])
                dst = bass.AP(stg_f, h2 * 4096, [[8192, 128], [cols, nchunk], [1, cols]])
                P("sync" if h2 == 0 else "scalar",
                  lambda q, s=src, d=dst: q.dma_start(d, s),
                  writes=[stgfb[h2]], dma=True)
                return h2, stg_f[:, h2 * 4096: h2 * 4096 + nchunk * cols]

            def load_convert(dram, rows, dst_ap, dst_buf):
                cols = dram.shape[1]
                nchunk = rows // 128
                assert nchunk * cols <= 4096
                h2, view_f = load_half(dram, 0, nchunk, cols)
                half_elems = nchunk * cols // 2
                tok = B(f"cvt_tok_{dram.name}")
                P("scalar", lambda q, a=dst_ap[:, 0:half_elems],
                  b=view_f[:, 0:half_elems]: q.activation(a, b, AF.Copy),
                  reads=[stgfb[h2]], writes=[dst_buf, tok])
                P("vector", lambda q, a=dst_ap[:, half_elems:2 * half_elems],
                  b=view_f[:, half_elems:2 * half_elems]: q.tensor_copy(a, b),
                  reads=[stgfb[h2], tok], writes=[dst_buf])

            def load_transpose(dram, rows, chW, dstT, dstT_buf, dst_cw):
                cols = chW
                nr = rows // 128
                nch = cols // 128
                nhalves = max((nr * cols + 4095) // 4096, (nr + 3) // 4)
                rows_per_half = nr // nhalves
                for hh in range(nhalves):
                    h2, view_f = load_half(dram, hh * rows_per_half * 128,
                                           rows_per_half, cols)
                    nn = rows_per_half * cols
                    vb1 = stg_b[:, h2 * 4096: h2 * 4096 + nn // 2]
                    vb2 = stg_b[:, h2 * 4096 + nn // 2: h2 * 4096 + nn]
                    tok = B(f"cvt_tok_b{h2}")
                    P("scalar", lambda q, a=vb1, b=view_f[:, 0:nn // 2]:
                      q.activation(a, b, AF.Copy),
                      reads=[stgfb[h2]], writes=[stgbb[h2], tok])
                    P("vector", lambda q, a=vb2, b=view_f[:, nn // 2:nn]:
                      q.tensor_copy(a, b),
                      reads=[stgfb[h2], tok], writes=[stgbb[h2]])
                    for ci in range(nch):
                        pt = ci % 2
                        for rj in range(rows_per_half):
                            src_t = stg_b[:, h2 * 4096 + rj * cols + ci * 128:
                                          h2 * 4096 + rj * cols + ci * 128 + 128]
                            dst_t = ppts[pt][:, rj * 128: rj * 128 + 128]
                            P("tensor",
                              lambda q, a=dst_t, b=src_t: q.transpose(a, b, ident[:]),
                              reads=[stgbb[h2], B("ident")], writes=[pptb[pt]],
                              accum=(rj > 0))
                        dst_sl = dstT[:, ci * dst_cw + hh * rows_per_half * 128:
                                      ci * dst_cw + (hh + 1) * rows_per_half * 128]
                        src_sl = ppts[pt][:, 0:rows_per_half * 128]
                        P("vector", lambda q, a=dst_sl, b=src_sl: q.tensor_copy(a, b),
                          reads=[pptb[pt]], writes=[dstT_buf])

            pi = 0
            def prep_mm(dst_ap, dst_buf, lhs_fn, rhs_fn, nk, reads, scale=None):
                nonlocal pi
                ps_t, ps_b = pps[pi % 2], ppb[pi % 2]
                pi += 1
                for k in range(nk):
                    P("tensor", lambda q, o=ps_t[:], l=lhs_fn(k), r=rhs_fn(k),
                      s1=(k == 0), s2=(k == nk - 1): q.matmul(o, l, r, start=s1, stop=s2),
                      reads=reads, writes=[ps_b], accum=(k > 0))
                if scale is None:
                    P("scalar", lambda q, a=dst_ap, b=ps_t[:]: q.activation(a, b, AF.Copy),
                      reads=[ps_b], writes=[dst_buf])
                else:
                    P("scalar", lambda q, a=dst_ap, b=ps_t[:], sc=scale:
                      q.activation(a, b, AF.Copy, scale=sc),
                      reads=[ps_b], writes=[dst_buf])

            load_transpose(wuq_d, 1024, 512, wuqT, B("wuqT"), 1024)
            load_convert(wdq_d, 512, wdq_st[:, 0:4 * 1024], B("wdq_st"))
            for ci in range(8):
                for h2 in range(2):
                    prep_mm(m1[:, ci * 1024 + h2 * 512: ci * 1024 + h2 * 512 + 512],
                            B("m1"),
                            lambda k, ci=ci: wdq_st[:, k * 1024 + ci * 128: k * 1024 + ci * 128 + 128],
                            lambda k, h2=h2: wuqT[:, k * 1024 + h2 * 512: k * 1024 + h2 * 512 + 512],
                            4, [B("wdq_st"), B("wuqT")])
            for ci in range(8):
                for h2 in range(2):
                    prep_mm(m1T[:, ci * 1024 + h2 * 512: ci * 1024 + h2 * 512 + 512],
                            B("m1T"),
                            lambda k, ci=ci: wuqT[:, k * 1024 + ci * 128: k * 1024 + ci * 128 + 128],
                            lambda k, h2=h2: wdq_st[:, k * 1024 + h2 * 512: k * 1024 + h2 * 512 + 512],
                            4, [B("wdq_st"), B("wuqT")])
            load_convert(wuk_d, 1024, wuk_st[:, 0:8 * 512], B("wuk_st"))
            for ri in range(8):
                prep_mm(keff[:, ri * 512: ri * 512 + 512], B("keff"),
                        lambda k, ri=ri: m1T[:, k * 1024 + ri * 128: k * 1024 + ri * 128 + 128],
                        lambda k: wuk_st[:, k * 512: k * 512 + 512],
                        8, [B("m1T"), B("wuk_st")], scale=0.125)
            load_convert(wuv_d, 1024, wuv_st[:, 0:8 * 512], B("wuv_st"))
            load_transpose(wdkv_d, 512, 1024, wdkvT, B("wdkvT"), 512)
            load_transpose(wo_d, 1024, 1024, woT, B("woT"), 1024)
            load_transpose(wr_d, 1024, 1024, wrT, B("wrT"), 1024)
            load_transpose(wk_d, 1024, 1024, wkT, B("wkT"), 1024)
            load_transpose(wv_d, 1024, 1024, wvT, B("wvT"), 1024)
            for li in range(4):
                for h2 in range(2):
                    prep_mm(vo[:, li * 1024 + h2 * 512: li * 1024 + h2 * 512 + 512],
                            B("vo"),
                            lambda k, li=li: wuv_st[:, k * 512 + li * 128: k * 512 + li * 128 + 128],
                            lambda k, h2=h2: woT[:, k * 1024 + h2 * 512: k * 1024 + h2 * 512 + 512],
                            8, [B("wuv_st"), B("woT")])

        ckvT = ec(nc.sbuf_tensor("ckvT", [128, 4 * 2048], BF16))
        ckvtok = ec(nc.sbuf_tensor("ckvtok", [128, 16 * 512], BF16))
        xnT = ec(nc.sbuf_tensor("xnT", [128, 8 * 512], BF16))
        qT = ec(nc.sbuf_tensor("qT", [128, 8 * 512], BF16))
        pcol0 = ec(nc.sbuf_tensor("pcol0", [128, 8], BF16))
        pcol1 = ec(nc.sbuf_tensor("pcol1", [128, 8], BF16))
        outacc = ec(nc.sbuf_tensor("outacc", [128, 4096], F32))
        qlat = ec(nc.sbuf_tensor("qlat", [128, 4 * 512], BF16))
        qlat1 = ec(nc.sbuf_tensor("qlat1", [128, 4 * 512], BF16))
        a_t0 = ec(nc.sbuf_tensor("a_t0", [128, 512], BF16))
        a_t1 = ec(nc.sbuf_tensor("a_t1", [128, 512], BF16))
        a_t2 = ec(nc.sbuf_tensor("a_t2", [128, 512], BF16))
        a_t3 = ec(nc.sbuf_tensor("a_t3", [128, 512], BF16))
        a_t4 = ec(nc.sbuf_tensor("a_t4", [128, 512], BF16))
        a_t5 = ec(nc.sbuf_tensor("a_t5", [128, 512], BF16))
        ylat = ec(nc.sbuf_tensor("ylat", [128, 4 * 512], BF16))
        recip = ec(nc.sbuf_tensor("recip", [128, 512], F32))
        scr512 = ec(nc.sbuf_tensor("scr512", [128, 512], F32))
        xr_g = ec(nc.sbuf_tensor("xr_g", [128, 8 * 512], BF16))
        xk_g = ec(nc.sbuf_tensor("xk_g", [128, 8 * 512], BF16))
        r_tile = recip
        bufs["r_tile"] = B("recip")
        kact = qT
        bufs["kact"] = B("qT")
        xt0 = ec(nc.sbuf_tensor("xt0", [128, 1024], F32))
        xt1 = ec(nc.sbuf_tensor("xt1", [128, 1024], F32))
        xn0 = ec(nc.sbuf_tensor("xn0", [128, 1024], BF16))
        xn1 = ec(nc.sbuf_tensor("xn1", [128, 1024], BF16))
        xn2 = ec(nc.sbuf_tensor("xn2", [128, 1024], BF16))
        xn3 = ec(nc.sbuf_tensor("xn3", [128, 1024], BF16))
        rc = ec(nc.sbuf_tensor("rc", [128, 8], F32))
        srow = recip
        bufs["srow"] = B("recip")
        stats = [[ec(nc.sbuf_tensor(f"st{n}{j}", [128, w], F32))
                  for n, w in (("bn", 12), ("ag", 2), ("rstd", 1), ("nbias", 1))]
                 for j in range(2)]
        statb = [[B(f"st{n}{j}") for n in ("bn", "ag", "rstd", "nbias")]
                 for j in range(2)]

        ps_a = ec(nc.psum_tensor("ps_a", [128, 512], F32))
        ps_b = ec(nc.psum_tensor("ps_b", [128, 512], F32))
        ps_y = [ec(nc.psum_tensor(f"ps_y{i}", [128, 512], F32)) for i in range(4)]
        ps_m = ec(nc.psum_tensor("ps_m", [128, 512], F32))
        ps_bc = ec(nc.psum_tensor("ps_bc", [128, 512], F32))

        pst = [ps_a, ps_b]
        psb = [B("ps_a"), B("ps_b")]
        pyb = [B(f"ps_y{i}") for i in range(4)]
        pst4 = [ps_a, ps_b, ps_y[0], ps_y[1]]
        psb4 = [B("ps_a"), B("ps_b"), pyb[0], pyb[1]]
        pmb = B("ps_m")
        pbcb = B("ps_bc")
        xts = [xt0, xt1]
        xtb = [B("xt0"), B("xt1")]
        xns = [xn0, xn1, xn2, xn3]
        xnb = [B("xn0"), B("xn1"), B("xn2"), B("xn3")]
        ats = [a_t0, a_t1, a_t2, a_t3, a_t4, a_t5]
        atb = [B("a_t0"), B("a_t1"), B("a_t2"), B("a_t3"), B("a_t4"), B("a_t5")]
        NAT = 6
        qlats = [qlat, qlat1]
        qlb = [[B(f"qlat{j}_{i}") for i in range(4)] for j in range(2)]
        ylb = [B(f"ylat{i}") for i in range(4)]
        ylb0s = [B(f"ylat0s{i}") for i in range(4)]
        pcols = [pcol0, pcol1]
        pcolb = [B("pcol0"), B("pcol1")]

        def M(eng, fn, reads=(), writes=(), dma=False, accum=False):
            return prog.op(eng, fn, reads, writes, dma=dma, accum=accum, phase=1)

        store_ids = []
        mm_ping = [0]

        def emit_mix_mult(g):
            xn3 = xnT[:].rearrange("p (c t) -> p c t", c=8)
            for dst, dstb, mu_t, mub in (
                    (xr_g, B("xr_g"), mu_r4, B("mu_r4")),
                    (xk_g, B("xk_g"), mu_k4, B("mu_k4"))):
                mu_bc = bass.AP(mu_t, 0, [[8, 128], [1, 8], [0, 512]])
                d3 = dst[:].rearrange("p (c t) -> p c t", c=8)
                M("gpsimd", lambda q, a=d3, b=xn3, c2=mu_bc:
                  q.tensor_tensor(a, b, c2, ALU.mult),
                  reads=[B("xnT"), mub], writes=[dstb])

        def emit_mix_chunk(g, ci):
            pc, pcb = pcols[g % 2], pcolb[g % 2]
            for dst, dstb, mu1m_t, mu1mb in (
                    (xr_g, B("xr_g"), mu1m_r4, B("mu1m_r4")),
                    (xk_g, B("xk_g"), mu1m_k4, B("mu1m_k4"))):
                o_ap = dst[:, ci * 512 + 1: ci * 512 + 512]
                i_ap = xnT[:, ci * 512: ci * 512 + 511]
                sc = mu1m_t[:, ci: ci + 1]
                M("vector", lambda q, a=o_ap, b=i_ap, s1=sc:
                  q.scalar_tensor_tensor(a, b, s1, a, ALU.mult, ALU.add),
                  reads=[B("xnT"), mu1mb, dstb], writes=[dstb])
                o0 = dst[:, ci * 512: ci * 512 + 1]
                p0_ap = pc[:, ci: ci + 1]
                M("vector", lambda q, a=o0, b=p0_ap, s1=sc:
                  q.scalar_tensor_tensor(a, b, s1, a, ALU.mult, ALU.add),
                  reads=[pcb, mu1mb, dstb], writes=[dstb])

        def pe_acc(dst_ap_fn, dst_buf, lhs_list, rhs_list, reads, nbank=2, off=0):
            ps_t, ps_bb = (pst4[off + mm_ping[0] % nbank],
                           psb4[off + mm_ping[0] % nbank])
            mm_ping[0] += 1
            n = len(lhs_list)
            for k in range(n):
                M("tensor", lambda q, o=ps_t[:], l=lhs_list[k], r=rhs_list[k],
                  s1=(k == 0), s2=(k == n - 1): q.matmul(o, l, r, start=s1, stop=s2),
                  reads=reads, writes=[ps_bb], accum=(k > 0))
            return ps_t, ps_bb


        iters = [(s, g) for s in range(B_LOC) for g in range(NGRP)]

        def emit_ln_tile(s, g, i):
            t0 = (g * NTIL + i) * 128
            xt, xb = xts[i % 2], xtb[i % 2]
            xn, xnbb = xns[i], xnb[i]
            (bn, ag, rstd, nbias) = stats[i % 2]
            (bn_b, ag_b, rstd_b, nbias_b) = statb[i % 2]
            M("sync", lambda q, d=xt[:], sl=x_d[s, t0:t0 + 128, :]:
              q.dma_start(d, sl), writes=[xb], dma=True)
            M("vector", lambda q, a=bn[:, 0:6], b=xt[:, 0:512]: q.bn_stats(a, b),
              reads=[xb], writes=[bn_b])
            M("vector", lambda q, a=bn[:, 6:12], b=xt[:, 512:1024]: q.bn_stats(a, b),
              reads=[xb], writes=[bn_b])
            M("vector", lambda q, a=ag[:], b=bn[:]: q.bn_aggr(a, b),
              reads=[bn_b], writes=[ag_b])
            M("vector", lambda q, a=ag[:, 1:2]: q.tensor_scalar_add(a, a, 1.0e-5),
              reads=[ag_b], writes=[ag_b])
            M("scalar", lambda q, a=ag[:, 1:2]: q.activation(a, a, AF.Ln),
              reads=[ag_b], writes=[ag_b])
            M("scalar", lambda q, a=rstd[:], b=ag[:, 1:2]:
              q.activation(a, b, AF.Exp, scale=-0.5),
              reads=[ag_b], writes=[rstd_b])
            M("vector", lambda q, a=nbias[:], b=ag[:, 0:1], c2=rstd[:]:
              q.scalar_tensor_tensor(a, b, -1.0, c2, ALU.mult, ALU.mult),
              reads=[ag_b, rstd_b], writes=[nbias_b])
            nbias_bc = bass.AP(nbias, 0, [[1, 128], [0, 1024]])
            M("vector", lambda q, a=xn[:], b=xt[:], nb=nbias_bc, sc=rstd[:]:
              q.scalar_tensor_tensor(a, b, sc, nb, ALU.mult, ALU.add),
              reads=[xb, nbias_b, rstd_b], writes=[xnbb])

        def emit_tr(i):
            xn, xnbb = xns[i], xnb[i]
            ps_t, ps_bb = pst[i % 2], psb[i % 2]
            ps_v = ps_t[:].bitcast(BF16)
            for ci in range(8):
                M("tensor", lambda q, a=ps_v[:, ci * 128:(ci + 1) * 128],
                  b=xn[:, ci * 128:(ci + 1) * 128]:
                  q.transpose(a, b, ident[:]),
                  reads=[xnbb, B("ident")], writes=[ps_bb],
                  accum=(ci > 0))
            src = ps_v.rearrange("p (c t) -> p c t", c=8)
            dst = bass.AP(xnT, i * 128, [[4096, 128], [512, 8], [1, 128]])
            M("vector", lambda q, a=dst, b=src: q.tensor_copy(a, b),
              reads=[ps_bb], writes=[B("xnT")])

        def emit_A(s, g, pre_tr=False):
            ckvT_b = B(f"ckvT_g{g}")
            ckvtok_b = B(f"ckvtok_g{g}")
            if not pre_tr:
                for i in range(NTIL):
                    emit_tr(i)
            if g < NGRP - 1:
                src = bass.AP(xnT, 511, [[4096, 128], [512, 8], [1, 1]])
                dst = bass.AP(pcols[(g + 1) % 2], 0, [[8, 128], [1, 8], [1, 1]])
                M("vector", lambda q, a=dst, b=src: q.tensor_copy(a, b),
                  reads=[B("xnT")], writes=[pcolb[(g + 1) % 2]])
            for lc in range(4):
                ps_t, ps_bb = pe_acc(None, None,
                    [wdkvT[:, ci * 512 + lc * 128: ci * 512 + lc * 128 + 128] for ci in range(8)],
                    [xnT[:, ci * 512: ci * 512 + 512] for ci in range(8)],
                    [B("wdkvT"), B("xnT")], nbank=4)
                dst = ckvT[:, lc * 2048 + g * 512: lc * 2048 + g * 512 + 512]
                M("scalar", lambda q, a=dst, b=ps_t[:]: q.activation(a, b, AF.Copy),
                  reads=[ps_bb], writes=[ckvT_b])
            for tt in range(NTIL):
                ps_t, ps_bb = pe_acc(None, None,
                    [xnT[:, ci * 512 + tt * 128: ci * 512 + tt * 128 + 128] for ci in range(8)],
                    [wdkvT[:, ci * 512: ci * 512 + 512] for ci in range(8)],
                    [B("wdkvT"), B("xnT")], nbank=4)
                dst = ckvtok[:, (g * NTIL + tt) * 512: (g * NTIL + tt) * 512 + 512]
                M("scalar", lambda q, a=dst, b=ps_t[:]: q.activation(a, b, AF.Copy),
                  reads=[ps_bb], writes=[ckvtok_b])
            for cqc in range(8):
                ps_t, ps_bb = pe_acc(None, None,
                    [m1[:, ci * 1024 + cqc * 128: ci * 1024 + cqc * 128 + 128] for ci in range(8)],
                    [xnT[:, ci * 512: ci * 512 + 512] for ci in range(8)],
                    [B("m1"), B("xnT")], nbank=4)
                dst = qT[:, cqc * 512: cqc * 512 + 512]
                M("vector", lambda q, a=dst, b=ps_t[:]: q.tensor_copy(a, b),
                  reads=[ps_bb], writes=[B("qT")])

        srow_ap = recip[0:1, 0:512]

        def emit_bc_rows(h):
            M("scalar", lambda q: q.activation(srow_ap, ps_m[0:1, 0:512], AF.Ln),
              reads=[pmb], writes=[B("recip")])
            M("scalar", lambda q: q.activation(srow_ap, srow_ap, AF.Exp, scale=-1.0),
              reads=[B("recip")], writes=[B("recip")])

        def emit_bc_rest(h):
            for sub in range(4):
                M("tensor", lambda q, o=ps_bc[:, sub:sub + 1],
                  l=srow_ap[:, sub * 128:(sub + 1) * 128],
                  s1=(sub == 0), s2=(sub == 3):
                  q.matmul(o, l, ones_row[0:1, 0:1], start=s1, stop=s2),
                  reads=[B("ones_row"), B("recip")], writes=[pbcb],
                  accum=(sub > 0))
            rcd = rc[:, (h % 2) * 4:(h % 2) * 4 + 4]
            M("vector", lambda q, a=rcd, b=ps_bc[:, 0:4]: q.tensor_copy(a, b),
              reads=[pbcb], writes=[B("rc")])
            for c4 in range(4):
                M("vector", lambda q, a=ylat[:, c4 * 128:(c4 + 1) * 128],
                  b=ps_y[0][:, c4 * 128:(c4 + 1) * 128]:
                  q.tensor_copy(a, b),
                  reads=[pyb[0]], writes=[ylb0s[c4]])
            for lc in range(1, 4):
                M("vector", lambda q, a=ylat[:, lc * 512:(lc + 1) * 512],
                  b=ps_y[lc][:]: q.tensor_copy(a, b),
                  reads=[pyb[lc]], writes=[ylb[lc]])

        def emit_y_lc(h, lc):
            for sub in range(4):
                rb = ylb0s[sub] if lc == 0 else ylb[lc]
                M("tensor", lambda q,
                  o=ps_m[:, sub * 64: sub * 64 + 64],
                  l=ylat[:, lc * 512 + sub * 128: lc * 512 + sub * 128 + 128],
                  r=vo[:, lc * 1024 + h * 64: lc * 1024 + h * 64 + 64],
                  s1=(lc == 0 and sub == 0), s2=(lc == 3 and sub == 3):
                  q.matmul(o, l, r, start=s1, stop=s2),
                  reads=[rb, B("vo")], writes=[pmb], accum=(lc > 0 or sub > 0))

        def emit_y_evac(h):
            ysrc = bass.AP(ps_m, 0, [[512, 128], [64, 4], [1, 64]])
            ydst = bass.AP(outacc, h * 64, [[4096, 128], [1024, 4], [1, 64]])
            rc_bc = bass.AP(rc, (h % 2) * 4, [[8, 128], [1, 4], [0, 64]])
            M("vector", lambda q, a=ydst, b=ysrc, c2=rc_bc:
              q.tensor_tensor(a, b, c2, ALU.mult),
              reads=[pmb, B("rc")], writes=[B("outacc")])

        def emit_y(h):
            for lc in range(4):
                emit_y_lc(h, lc)
            emit_y_evac(h)

        def emit_k_ffn(g):
            for dkc in range(8):
                ps_t, ps_bb = pe_acc(None, None,
                    [wkT[:, ci * 1024 + dkc * 128: ci * 1024 + dkc * 128 + 128] for ci in range(8)],
                    [xk_g[:, ci * 512: ci * 512 + 512] for ci in range(8)],
                    [B("wkT"), B("xk_g")], nbank=4)
                M("scalar", lambda q, a=scr512[:], b=ps_t[:]: q.activation(a, b, AF.Relu),
                  reads=[ps_bb], writes=[B("scr512")])
                M("scalar", lambda q, a=kact[:, dkc * 512:(dkc + 1) * 512], b=scr512[:]:
                  q.activation(a, b, AF.Square),
                  reads=[B("scr512")], writes=[B("kact")])

        def emit_B(s, g, early_ln):
            K = 4 * g + 4
            LEAD = 2
            pend = [None]

            def scores(h, kt):
                q0 = max(0, (kt - 4 * g) * 128)
                ps_t, ps_bb = pst[kt % 2], psb[kt % 2]
                ckv_rb = B(f"ckvT_g{kt // 4}")
                ql_t = qlats[h % 2]
                for lc in range(4):
                    lhs = ckvT[:, lc * 2048 + kt * 128: lc * 2048 + kt * 128 + 128]
                    rhs = ql_t[:, lc * 512 + q0: lc * 512 + 512]
                    M("tensor", lambda q, o=ps_t[:, q0:512], l=lhs, r=rhs,
                      s1=(lc == 0), s2=(lc == 3):
                      q.matmul(o, l, r, start=s1, stop=s2),
                      reads=[ckv_rb, qlb[h % 2][lc]], writes=[ps_bb], accum=(lc > 0))

            def qlat_stage(hh, lc, bank=None, evac="vector"):
                p0 = (hh % 2) * 64
                hc = hh // 2
                ps_t, ps_bb = (ps_bc, pbcb) if bank is None else bank
                lhs = keff[p0:p0 + 64, hc * 512 + lc * 128: hc * 512 + lc * 128 + 128]
                rhs = qT[p0:p0 + 64, hc * 512: hc * 512 + 512]
                M("tensor", lambda q, o=ps_t[:], l=lhs, r=rhs:
                  q.matmul(o, l, r, start=True, stop=True),
                  reads=[B("keff"), B("qT")], writes=[ps_bb])
                ql_t = qlats[hh % 2]
                if evac == "vector":
                    M("vector", lambda q, a=ql_t[:, lc * 512:(lc + 1) * 512],
                      b=ps_t[:]: q.tensor_copy(a, b),
                      reads=[ps_bb], writes=[qlb[hh % 2][lc]])
                else:
                    M("scalar", lambda q, a=ql_t[:, lc * 512:(lc + 1) * 512],
                      b=ps_t[:]: q.activation(a, b, AF.Copy),
                      reads=[ps_bb], writes=[qlb[hh % 2][lc]])

            def maskexp(kt):
                q0 = max(0, (kt - 4 * g) * 128)
                ps_t, ps_bb = pst[kt % 2], psb[kt % 2]
                at, ab = ats[kt % NAT], atb[kt % NAT]
                M("scalar", lambda q, a=at[:, q0:512], b=ps_t[:, q0:512]:
                  q.activation(a, b, AF.Exp),
                  reads=[ps_bb], writes=[ab])
                if kt >= 4 * g:
                    M("vector", lambda q, a=at[:, q0:q0 + 128]:
                      q.tensor_tensor(a, a, negmask[:], ALU.mult),
                      reads=[ab, B("negmask")], writes=[ab])

            def ylat_only(kt):
                q0 = max(0, (kt - 4 * g) * 128)
                at, ab = ats[kt % NAT], atb[kt % NAT]
                ctok_rb = B(f"ckvtok_g{kt // 4}")
                for lc in range(4):
                    lhs = ckvtok[:, kt * 512 + lc * 128: kt * 512 + lc * 128 + 128]
                    M("tensor", lambda q, o=ps_y[lc][:, q0:512], l=lhs,
                      r=at[:, q0:512], s1=(kt == 0), s2=(kt == K - 1):
                      q.matmul(o, l, r, start=s1, stop=s2),
                      reads=[ctok_rb, ab], writes=[pyb[lc]], accum=(kt > 0))

            def sums_only(kt):
                q0 = max(0, (kt - 4 * g) * 128)
                at, ab = ats[kt % NAT], atb[kt % NAT]
                M("tensor", lambda q, o=ps_m[0:1, q0:512], l=ones_col[:],
                  r=at[:, q0:512], s1=(kt == 0), s2=(kt == K - 1):
                  q.matmul(o, l, r, start=s1, stop=s2),
                  reads=[B("ones_col"), ab], writes=[pmb], accum=(kt > 0))

            def ylat_mm(kt):
                ylat_only(kt)
                sums_only(kt)

            for h in range(NH):
                if h == 0:
                    for lc in range(4):
                        qlat_stage(0, lc)
                for hh, fn in early_ln:
                    if hh == h:
                        fn()
                scores(h, 0)
                maskexp(0)
                scores(h, 1)
                maskexp(1)
                if pend[0] is not None:
                    emit_bc_rest(pend[0])
                scores(h, 2)
                maskexp(2)
                scores(h, 3)
                maskexp(3)
                if K == 4 and h + 1 < NH:
                    g0banks = [None, (ps_m, pmb), (ps_a, psb[0]), (ps_b, psb[1])]
                    for lc in range(4):
                        qlat_stage(h + 1, lc, bank=g0banks[lc],
                                   evac="scalar" if lc % 2 == 0 else "vector")
                if pend[0] is not None:
                    emit_y(pend[0])
                    pend[0] = None
                nyl = [0]

                def after_ylat():
                    if K > 4 and h + 1 < NH and nyl[0] < 4:
                        qlat_stage(h + 1, nyl[0])
                    nyl[0] += 1
                LEAD = 5 if K > 4 else 4
                if K > 4:
                    scores(h, 4)
                    maskexp(4)
                for kt in range(LEAD, K):
                    scores(h, kt)
                    maskexp(kt)
                    ylat_mm(kt - LEAD)
                    after_ylat()
                for kt in range(max(0, K - LEAD), K):
                    sums_only(kt)
                emit_bc_rows(h)
                for kt in range(max(0, K - LEAD), K):
                    ylat_only(kt)
                    after_ylat()
                pend[0] = h
            emit_bc_rest(pend[0])
            emit_k_ffn(g)
            emit_y(pend[0])
            pend[0] = None

        def emit_C(s, g, interleave_tr=False):
            for tt in range(NTIL):
                t0 = (g * NTIL + tt) * 128
                xt, xb = xts[tt % 2], xtb[tt % 2]
                M("sync", lambda q, d=xt[:], sl=x_d[s, t0:t0 + 128, :]:
                  q.dma_start(d, sl), writes=[xb], dma=True)
                for dh in range(2):
                    ps_t, ps_bb = pe_acc(None, None,
                        [xr_g[:, ci * 512 + tt * 128: ci * 512 + tt * 128 + 128] for ci in range(8)],
                        [wrT[:, ci * 1024 + dh * 512: ci * 1024 + dh * 512 + 512] for ci in range(8)],
                        [B("wrT"), B("xr_g")], nbank=2, off=2)
                    rh = r_tile[:, 0:512]
                    M("scalar", lambda q, a=rh, b=ps_t[:]:
                      q.activation(a, b, AF.Exp, scale=-1.0),
                      reads=[ps_bb], writes=[B("r_tile")])
                    M("vector", lambda q, a=rh: q.tensor_scalar_add(a, a, 1.0),
                      reads=[B("r_tile")], writes=[B("r_tile")])
                    M("scalar", lambda q, a=rh: q.activation(a, a, AF.Ln),
                      reads=[B("r_tile")], writes=[B("r_tile")])
                    M("scalar", lambda q, a=rh: q.activation(a, a, AF.Exp, scale=-1.0),
                      reads=[B("r_tile")], writes=[B("r_tile")])
                    ps_t, ps_bb = pe_acc(None, None,
                        [kact[:, dkc * 512 + tt * 128: dkc * 512 + tt * 128 + 128] for dkc in range(8)],
                        [wvT[:, dkc * 1024 + dh * 512: dkc * 1024 + dh * 512 + 512] for dkc in range(8)],
                        [B("wvT"), B("kact")], nbank=2, off=2)
                    M("vector", lambda q, a=scr512[:], b=rh, c2=ps_t[:]:
                      q.tensor_tensor(a, b, c2, ALU.mult),
                      reads=[B("r_tile"), ps_bb], writes=[B("scr512")])
                    oa = outacc[:, tt * 1024 + dh * 512: tt * 1024 + dh * 512 + 512]
                    M("vector", lambda q, a=oa: q.tensor_tensor(a, a, scr512[:], ALU.add),
                      reads=[B("scr512"), B("outacc")], writes=[B("outacc")])
                oa = outacc[:, tt * 1024: tt * 1024 + 1024]
                M("vector", lambda q, a=oa, b=xt[:]: q.tensor_tensor(a, a, b, ALU.add),
                  reads=[xb, B("outacc")], writes=[B("outacc")])
                if interleave_tr:
                    emit_tr(tt)
            g0t = g * TG
            dsts = bass.AP(out_d.tensor, (s * T + g0t) * 1024,
                           [[1024, 128], [128 * 1024, 4], [1, 1024]])
            srcs = outacc[:].rearrange("p (c t) -> p c t", c=4)
            sid = M("scalar", lambda q, d=dsts, b=srcs: q.dma_start(d, b),
                    reads=[B("outacc")], dma=True)
            store_ids.append(sid)

        for i in range(NTIL):
            emit_ln_tile(iters[0][0], iters[0][1], i)
        for it, (s, g) in enumerate(iters):
            if g == 0:
                M("vector", lambda q: q.memset(pcol0[:], 0.0), writes=[pcolb[0]])
            emit_A(s, g, pre_tr=(it > 0))
            if DEBUG_DUMP and it == 0:
                sid = M("scalar", lambda q: q.dma_start(dbg_d[:, :], xnT[:]),
                        reads=[B("xnT")], dma=True)
                store_ids.append(sid)
            early = [(1, lambda gg=g: emit_mix_mult(gg))]
            for ci in range(8):
                early.append((2 + ci, lambda gg=g, cc=ci: emit_mix_chunk(gg, cc)))
            if it + 1 < len(iters):
                s2, g2 = iters[it + 1]
                for i in range(NTIL):
                    early.append((6 + 2 * i,
                                  lambda ss=s2, gg=g2, ii=i: emit_ln_tile(ss, gg, ii)))
            emit_B(s, g, early)
            if DEBUG_DUMP and it == 0:
                sid = M("scalar", lambda q: q.dma_start(dbgo_d[:, :], outacc[:]),
                        reads=[B("outacc")], dma=True)
                store_ids.append(sid)
                sid = M("scalar", lambda q: q.dma_start(dbgr_d[:, :], recip[:]),
                        reads=[B("recip")], dma=True)
                store_ids.append(sid)
                sid = M("scalar", lambda q: q.dma_start(dbgy_d[:, :], ylat[:]),
                        reads=[ylb[0], ylb[1], ylb[2], ylb[3]], dma=True)
                store_ids.append(sid)
            emit_C(s, g, interleave_tr=(it + 1 < len(iters)))

        prog.finalize(extra_targets=store_ids)

        sems = {}
        for key in [("tensor", False), ("vector", False), ("scalar", False),
                    ("gpsimd", False), ("sync", True), ("scalar", True),
                    ("sync", False)]:
            sems[key] = ec(nc.semaphore(f"sem_{key[0]}_{'d' if key[1] else 'c'}"))

        with nc.Block() as b0:
            for eng in Prog.ENGINES:
                getattr(b0, eng)(prog.runner(eng, 0, sems))
        with nc.Block() as b1:
            for eng in Prog.ENGINES:
                getattr(b1, eng)(prog.runner(eng, 1, sems))
        sc_total = prog.sem_totals[("scalar", True)]
        with nc.Block() as b2:
            @b2.scalar
            def _(q):
                q.wait_ge(sems[("scalar", True)], sc_total)

    return nc


def kernel(**inputs):
    x = np.ascontiguousarray(inputs["L0_attn_x"], dtype=np.float32)
    nc = build_nc()
    weights = dict(
        wdq=np.ascontiguousarray(inputs["L0_attn_W_dq"], dtype=np.float32),
        wuq=np.ascontiguousarray(inputs["L0_attn_W_uq"], dtype=np.float32),
        wdkv=np.ascontiguousarray(inputs["L0_attn_W_dkv"], dtype=np.float32),
        wuk=np.ascontiguousarray(inputs["L0_attn_W_uk"], dtype=np.float32),
        wuv=np.ascontiguousarray(inputs["L0_attn_W_uv"], dtype=np.float32),
        wo=np.ascontiguousarray(inputs["L0_attn_W_o"], dtype=np.float32),
        wr=np.ascontiguousarray(inputs["L0_ffn_W_r"], dtype=np.float32),
        wk=np.ascontiguousarray(inputs["L0_ffn_W_k"], dtype=np.float32),
        wv=np.ascontiguousarray(inputs["L0_ffn_W_v"], dtype=np.float32),
        mur=np.ascontiguousarray(inputs["L0_ffn_mu_r"], dtype=np.float32),
        muk=np.ascontiguousarray(inputs["L0_ffn_mu_k"], dtype=np.float32),
    )
    in_maps = []
    for c in range(8):
        m = dict(weights)
        m["x"] = x[c * B_LOC:(c + 1) * B_LOC]
        in_maps.append(m)
    res = run_bass_kernel_spmd(nc, in_maps, core_ids=list(range(8)))
    return np.concatenate([r["out"] for r in res.results], axis=0)
